# version 32
# baseline (speedup 1.0000x reference)
import contextlib
import numpy as np
import concourse.bass as bass
import concourse.mybir as mybir
from concourse.alu_op_type import AluOpType
from concourse.bass_utils import run_bass_kernel_spmd

F32 = mybir.dt.float32
BF16 = mybir.dt.bfloat16
I8 = mybir.dt.int8
AF = mybir.ActivationFunctionType
AX = mybir.AxisListType.X
Mul = AluOpType.mult
Add = AluOpType.add
Sub = AluOpType.subtract
MaxO = AluOpType.max

NCORE = 8
D = 1024
T = 2048
IFF = 2816
DC = D // 128
NJ = DC
NI = IFF // 128
NTT = T // 128
NT4 = T // 512

ENGINES = ("sync", "gpsimd", "tensor", "vector", "scalar")
OWNED = {"sP": "tensor", "sV": "vector", "sA": "scalar"}


class Sched:
    def __init__(self):
        self.ops = {e: [] for e in ENGINES}
        self.cnt = {s: 0 for s in ("sL", "sS", "sG", "sP", "sV", "sA", "sC", "sW0", "sW1", "sW2", "sW3")}
        self.psum_read = {b: None for b in range(8)}

    def op(self, eng, fn, waits=(), inc=None, amt=1, drain=False, dma=False):
        waits = tuple(w for w in waits if w is not None)
        tok = None
        if inc is not None:
            self.cnt[inc] += amt
            tok = (inc, self.cnt[inc])
            if eng == "scalar" and not dma:
                drain = True
        self.ops[eng].append((fn, waits, inc, amt, drain))
        return tok

    def tok(self, sem):
        return (sem, self.cnt[sem])

    def pe(self, fn, waits=(), banks=(), drain=True):
        w = list(waits) + [self.psum_read[b] for b in banks]
        return self.op("tensor", fn, waits=w, inc="sP", drain=drain)

    def rd(self, eng, fn, waits=(), banks=(), inc=None, drain=False):
        if inc is None:
            inc = {"vector": "sV", "scalar": "sA"}[eng]
        t = self.op(eng, fn, waits=waits, inc=inc, drain=drain)
        for b in banks:
            self.psum_read[b] = t
        return t

    def vtiny(self, fn, waits=(), inc="sV"):
        return self.op("vector", fn, waits=waits, inc=inc, drain=True)

    def emit(self, block, sems):
        def section(eng_name):
            def run(eng):
                waited = {}
                for fn, waits, inc, amt, drain in self.ops[eng_name]:
                    for (s, v) in waits:
                        if OWNED.get(s) == eng_name:
                            continue
                        if waited.get(s, -1) >= v:
                            continue
                        eng.wait_ge(sems[s], v)
                        waited[s] = v
                    ins = fn(eng)
                    if drain:
                        eng.drain()
                    if inc is not None:
                        if ins is not None and not drain:
                            ins.then_inc(sems[inc], amt)
                        else:
                            eng.nop().then_inc(sems[inc], amt)
            return run

        for name in ENGINES:
            if self.ops[name]:
                getattr(block, name)(section(name))


def build_nc(debug=False):
    nc = bass.Bass("TRN2", target_bir_lowering=False)

    xin = nc.dram_tensor("xin", [D, T], F32, kind="ExternalInput")
    wg = nc.dram_tensor("wg", [IFF, D], F32, kind="ExternalInput")
    wu = nc.dram_tensor("wu", [IFF, D], F32, kind="ExternalInput")
    wd = nc.dram_tensor("wd", [D, IFF], F32, kind="ExternalInput")
    qkw = nc.dram_tensor("qkw", [D, D], F32, kind="ExternalInput")
    vow = nc.dram_tensor("vow", [D, D], F32, kind="ExternalInput")
    clx = nc.dram_tensor("clx", [1, 1], F32, kind="ExternalInput")
    cld = nc.dram_tensor("cld", [1, 1], F32, kind="ExternalInput")
    y = nc.dram_tensor("y", [D, T], F32, kind="ExternalOutput")
    wgq2 = nc.dram_tensor("wgq2", [NI, 128, D], BF16)
    wuq2 = nc.dram_tensor("wuq2", [NI, 128, D], BF16)
    wdq2 = nc.dram_tensor("wdq2", [NJ, 128, IFF], BF16)
    cin = nc.dram_tensor("cin", [128, 1], F32)
    cout = nc.dram_tensor("cout", [128, 1], F32)
    gsc = nc.dram_tensor("gsc", [1, 1], F32)
    if debug:
        dY1 = nc.dram_tensor("dY1", [D, T], F32, kind="ExternalOutput")
        dY2 = nc.dram_tensor("dY2", [D, T], F32, kind="ExternalOutput")
        dSC = nc.dram_tensor("dSC", [128, 8], F32, kind="ExternalOutput")
        dRM = nc.dram_tensor("dRM", [128, 80], F32, kind="ExternalOutput")
        dSG = nc.dram_tensor("dSG", [128, 66], F32, kind="ExternalOutput")

    S = Sched()

    with contextlib.ExitStack() as ctx:
        ec = ctx.enter_context
        xT = ec(nc.sbuf_tensor("xT", [128, DC, T], F32))
        lnb = ec(nc.sbuf_tensor("lnb", [128, DC, T], BF16))
        slab = ec(nc.sbuf_tensor("slab", [128, 52736], BF16))
        i8b = ec(nc.sbuf_tensor("i8b", [128, 2, IFF], I8))
        sgw = ec(nc.sbuf_tensor("sgw", [128, 3, 22], F32))
        amaxc = ec(nc.sbuf_tensor("amaxc", [128, 32], F32))
        vecs = ec(nc.sbuf_tensor("vecs", [128, 16], F32))
        ones128 = ec(nc.sbuf_tensor("ones128", [128, 128], BF16))
        ident = ec(nc.sbuf_tensor("ident", [128, 128], BF16))
        crow = ec(nc.sbuf_tensor("crow", [128, 128], F32))
        sml = ec(nc.sbuf_tensor("sml", [128, 16], F32))
        rmw = ec(nc.sbuf_tensor("rmw", [128, 80], F32))
        invw = ec(nc.sbuf_tensor("invw", [128, 64], F32))
        dbgt = ec(nc.sbuf_tensor("dbgt", [128, 8], F32)) if debug else None
        PS = ec(nc.psum_tensor("PS", [128, 8, 512], F32))
        semL = ec(nc.semaphore("sL"))
        semG = ec(nc.semaphore("sG"))
        semP = ec(nc.semaphore("sP"))
        semV = ec(nc.semaphore("sV"))
        semA = ec(nc.semaphore("sA"))
        semC = ec(nc.semaphore("sC"))
        semS = ec(nc.semaphore("sS"))
        semW = [ec(nc.semaphore("sW%d" % i)) for i in range(4)]
        block = ec(nc.Block())
        sems = {"sL": semL, "sG": semG, "sP": semP, "sV": semV, "sA": semA, "sC": semC,
                "sS": semS,
                "sW0": semW[0], "sW1": semW[1], "sW2": semW[2], "sW3": semW[3]}

        def sl(off, n):
            return slab[:, off:off + n]

        hv = [sl(k * T, T) for k in range(NI)]
        ln_tok = sl(0, 16384).rearrange("p (u d) -> p u d", d=1024)
        qc = sl(16384, 4096).rearrange("p (e m) -> p e m", m=512)
        Praw = sl(20480, 2048)
        Pn = sl(22528, 2048)
        PT = sl(24576, 8192).rearrange("p (u m) -> p u m", m=512)
        avbuf = sl(32768, 4096).rearrange("p (d m) -> p d m", m=512)
        xbf = sl(36864, 4096).rearrange("p (s m) -> p s m", s=4)
        x2bf = sl(40960, 4096).rearrange("p (s m) -> p s m", s=4)
        wwin_gu = sl(45056, 4096).rearrange("p (b w m) -> p b w m", b=2, w=2)
        wwin_d = sl(45056, 5632).rearrange("p (b m) -> p b m", b=2)
        wwin_qk = sl(45056, 4096).rearrange("p (b m) -> p b m", b=4)
        MB = sl(45056, 2048).bitcast(F32)
        RB = sl(47104, 2048).bitcast(F32)
        t32 = sl(49152, 2048).bitcast(F32)
        sgf2 = sl(50688, 2048).bitcast(F32)
        sgf = [sgf2[:, 0:512], sgf2[:, 512:1024]]

        dbg = sml
        c1024 = sml[:, 1:2]
        epsc = sml[:, 2:3]
        sigx = sml[:, 3:4]
        sigd = sml[:, 4:5]
        invq = sml[:, 5:6]
        sclq = sml[:, 6:7]
        c32 = sml[:, 7:8]
        hscl = sml[:, 8:9]
        tmp1 = sml[:, 9:10]
        tmp2 = sml[:, 10:11]

        S.op("vector", lambda v: v.memset(sml[:, :], 0.0))
        S.op("vector", lambda v: v.memset(c1024, 1.0 / 1024.0))
        S.op("vector", lambda v: v.memset(epsc, 1e-5))
        S.op("vector", lambda v: v.memset(c32, 1.0 / 32.0))
        S.op("vector", lambda v: v.memset(ones128[:, :], 1.0))
        S.op("gpsimd", lambda g: g.memset(ident[:, :], 0.0))
        t_id = S.op("gpsimd", lambda g: g.affine_select(
            out=ident[:, :], in_=ident[:, :], compare_op=AluOpType.not_equal,
            fill=1.0, base=0, pattern=[[-1, 128]], channel_multiplier=1), inc="sC")

        t_cl = S.op("gpsimd", lambda g: g.dma_start(
            out=sml[:, 11:12], in_=bass.AP(clx, 0, [[0, 128], [1, 1]])), inc="sG", amt=16)
        t_cl2 = S.op("gpsimd", lambda g: g.dma_start(
            out=sml[:, 12:13], in_=bass.AP(cld, 0, [[0, 128], [1, 1]])), inc="sG", amt=16)
        sig_tok_x = S.op("scalar", lambda s: s.activation(sigx, sml[:, 11:12], AF.Sigmoid),
                         waits=[t_cl], inc="sA", drain=True)
        sig_tok_d = S.op("scalar", lambda s: s.activation(sigd, sml[:, 12:13], AF.Sigmoid),
                         waits=[t_cl2], inc="sA", drain=True)

        t_x = [S.op("scalar", lambda s, c=c: s.dma_start(
            out=xT[:, c, :], in_=xin[128 * c:128 * (c + 1), :]), inc="sS", amt=16,
            dma=True) for c in range(DC)]

        state = {"tmp_guard": None, "coll_guard": None, "ffn_act_guard": None}
        prep_ctr = [0]
        prep_slot_guard = {}
        sgwf = sgw.ap().rearrange("p a b -> p (a b)")

        def emit_prep():
            pk = rmw
            pk2 = invw
            prep_bank = [0]
            slot_guard = {}

            def quant_tail(n, ld, cols, mult_ap, dst_ap, sc_tok):
                slot = n % 8
                base = 4096 * slot
                qb = sl(base + 2048, 1024)
                tb = sl(base + 3072, 1024)
                g = slot_guard.get(slot, (None, None, None))
                i8s = n % 2
                S.op("scalar", lambda s, ld=ld, cols=cols, i8s=i8s, mult_ap=mult_ap: s.activation(
                    i8b[:, i8s, 0:cols], ld[:, 0:cols], AF.Identity,
                    bias=0.0, scale=mult_ap),
                    waits=[sc_tok, state.get(("i8_guard", i8s))], drain=True)
                t_b = S.op("scalar", lambda s, qb=qb, cols=cols, i8s=i8s: s.activation(
                    qb[:, 0:cols], i8b[:, i8s, 0:cols], AF.Identity),
                    waits=[g[1]], inc="sA", drain=True)
                state[("i8_guard", i8s)] = t_b
                bank = 4 + (prep_bank[0] % 4)
                prep_bank[0] += 1
                pb = PS[:, bank, :].bitcast(BF16)
                t_tr = S.pe(lambda t, qb=qb, pb=pb, nblk=cols // 128: [
                    t.transpose(pb[:, 128 * m:128 * (m + 1)],
                                qb[:, 128 * m:128 * (m + 1)], ident[:, :])
                    for m in range(cols // 128)][-1],
                    waits=[t_b, t_id], banks=(bank,))
                t_ev = S.rd("vector", lambda v, tb=tb, cols=cols, pb=pb: v.tensor_copy(
                    tb[:, 0:cols], pb[:, 0:cols]),
                    waits=[t_tr, g[2]], banks=(bank,), drain=True)
                t_s = S.op("sync", lambda sy, dst_ap=dst_ap, tb=tb, cols=cols: sy.dma_start(
                    out=dst_ap, in_=tb[:, 0:cols]), waits=[t_ev], inc="sL", amt=16)
                slot_guard[slot] = (t_s, t_tr, t_s)
                return t_s

            def load(n, src_ap, cols):
                slot = n % 8
                ld = sl(4096 * slot, 2048).bitcast(F32)
                g = slot_guard.get(slot, (None, None, None))
                t_l = S.op("gpsimd", lambda gp, src_ap=src_ap, ld=ld, cols=cols: gp.dma_start(
                    out=ld[:, 0:cols], in_=src_ap), waits=[g[0]],
                    inc="sW%d" % (n % 4), amt=16, dma=True)
                return ld, t_l

            gate_spill = [None] * NI
            up_spill = [None] * NI
            n = 0
            for kp in range(11):
                items = []
                for kk in range(2):
                    k = 2 * kp + kk
                    for widx, wsrc in ((0, wg), (1, wu)):
                        col = 2 * k + widx
                        ld, t_l = load(n + len(items), wsrc[128 * k:128 * (k + 1), :], 1024)
                        t_r = S.op("vector", lambda v, ld=ld, col=col: v.reduce_max(
                            pk[:, col:col + 1], ld[:, 0:1024], axis=AX,
                            apply_absolute_value=True),
                            waits=[t_l], inc="sV", drain=(len(items) == 3))
                        items.append((ld, col, t_r))
                c0 = 4 * kp
                S.op("vector", lambda v, c0=c0: v.reciprocal(
                    pk2[:, c0:c0 + 4], pk[:, c0:c0 + 4]),
                    waits=[items[-1][2]], inc="sV", drain=True)
                sc_tok = S.op("vector", lambda v, c0=c0: v.tensor_scalar(
                    pk2[:, c0:c0 + 4], pk2[:, c0:c0 + 4], 127.0, 0.0, Mul, Add),
                    inc="sV", drain=True)
                for idx, (ld, col, _) in enumerate(items):
                    k = 2 * kp + idx // 2
                    widx = idx % 2
                    dst = (wgq2 if widx == 0 else wuq2).ap()[k, :, :]
                    t_s = quant_tail(n + idx, ld, 1024, pk2[:, col:col + 1], dst, sc_tok)
                    if widx == 0:
                        gate_spill[k] = t_s
                    else:
                        up_spill[k] = t_s
                n += 4

            down_spill = []
            for j in range(NJ):
                items = []
                for cc in range(3):
                    w = min(1024, IFF - 1024 * cc)
                    col = 44 + 3 * j + cc
                    ld, t_l = load(n + cc, wd[128 * j:128 * (j + 1), 1024 * cc:1024 * cc + w], w)
                    t_r = S.op("vector", lambda v, ld=ld, col=col, w=w: v.reduce_max(
                        pk[:, col:col + 1], ld[:, 0:w], axis=AX,
                        apply_absolute_value=True),
                        waits=[t_l], inc="sV", drain=(cc == 2))
                    items.append((ld, w, t_r))
                S.vtiny(lambda v, j=j: v.reduce_max(
                    pk[:, 68 + j:69 + j], pk[:, 44 + 3 * j:47 + 3 * j], axis=AX),
                    waits=[items[-1][2]])
                S.op("vector", lambda v, j=j: v.reciprocal(
                    pk2[:, 44 + j:45 + j], pk[:, 68 + j:69 + j]), inc="sV", drain=True)
                sc_tok = S.op("vector", lambda v, j=j: v.tensor_scalar(
                    pk2[:, 44 + j:45 + j], pk2[:, 44 + j:45 + j], 127.0, 0.0, Mul, Add),
                    inc="sV", drain=True)
                last = None
                for cc, (ld, w, _) in enumerate(items):
                    last = quant_tail(n + cc, ld, w, pk2[:, 44 + j:45 + j],
                                      wdq2.ap()[j, :, 1024 * cc:1024 * cc + w], sc_tok)
                n += 3
                down_spill.append(last)
            gca = bass.AP(pk[:, :].tensor, 0, [[80, 128], [2, 22]])
            uca = bass.AP(pk[:, :].tensor, 1, [[80, 128], [2, 22]])
            S.op("vector", lambda v, gca=gca: v.tensor_scalar(
                sgwf[:, 0:22], gca, 1.0 / 127.0, 0.0, Mul, Add), inc="sV", drain=True)
            S.op("vector", lambda v, uca=uca: v.tensor_scalar(
                sgwf[:, 22:44], uca, 1.0 / 127.0, 0.0, Mul, Add), inc="sV", drain=True)
            S.op("vector", lambda v: v.tensor_scalar(
                sgwf[:, 44:52], pk[:, 68:76], 1.0 / 127.0, 0.0, Mul, Add),
                inc="sV", drain=True)
            return gate_spill, up_spill, down_spill


        def bounce_allreduce(src_ap, waits):
            t1 = S.op("gpsimd", lambda g: g.dma_start(out=cin[:, :], in_=src_ap),
                      waits=list(waits) + [state["coll_guard"]], inc="sG", amt=16)
            t2 = S.op("gpsimd", lambda g: g.collective_compute(
                "AllReduce", MaxO, replica_groups=[list(range(NCORE))],
                ins=[cin.ap()], outs=[cout.ap()]), waits=[t1], inc="sC")
            state["coll_guard"] = t2
            t3 = S.op("gpsimd", lambda g: g.dma_start(
                out=crow[:, :], in_=bass.AP(cout, 0, [[0, 128], [1, 128]])),
                waits=[t2, state["tmp_guard"]], inc="sG", amt=16)
            t4 = S.vtiny(lambda v: v.reduce_max(tmp1, crow[:, :], axis=AX),
                         waits=[t3])
            return t4

        def quant_scales(sig_ap, gmax_tok, sig_tok, dbg_col=None):
            t1 = S.vtiny(lambda v, sig_ap=sig_ap: v.tensor_tensor(
                tmp2, tmp1, sig_ap, Mul), waits=[gmax_tok, sig_tok])
            state["tmp_guard"] = t1
            S.vtiny(lambda v: v.tensor_scalar(sclq, tmp2, 1.0 / 127.0, 0.0, Mul, Add))
            t3 = S.vtiny(lambda v: v.reciprocal(invq, sclq))
            if debug and dbg_col is not None:
                S.op("vector", lambda v, c=dbg_col: v.tensor_copy(
                    dbgt[:, c:c + 1], sclq), inc="sV")
                S.op("vector", lambda v, c=dbg_col: v.tensor_copy(
                    dbgt[:, c + 1:c + 2], tmp1), inc="sV")
            return t3

        def layer_norm(xtoks, with_amax):
            apply_toks = [None] * DC
            xslot_guard = {}
            x2slot_guard = {}
            for half in range(2):
                h0 = 1024 * half
                mm_x_last = mm_x2_last = None
                for c in range(DC):
                    slot = c % 4
                    t_cast = S.op("vector", lambda v, c=c, slot=slot, h0=h0: v.tensor_copy(
                        xbf[:, slot, :], xT[:, c, h0:h0 + 1024]),
                        waits=list(xtoks) + [xslot_guard.get(slot)], inc="sV", drain=True)
                    t_sq = S.op("scalar", lambda s, c=c, slot=slot, h0=h0: s.activation(
                        x2bf[:, slot, :], xT[:, c, h0:h0 + 1024], AF.Square),
                        waits=list(xtoks) + [x2slot_guard.get(slot)], inc="sA")
                    mm_x_last = S.pe(lambda t, c=c, slot=slot: [
                        t.matmul(PS[:, q, :], ones128[:, :], xbf[:, slot, 512 * q:512 * (q + 1)],
                                 start=(c == 0), stop=(c == DC - 1)) for q in range(2)][-1],
                        waits=[t_cast], banks=(0, 1) if c == 0 else ())
                    xslot_guard[slot] = mm_x_last
                    mm_x2_last = S.pe(lambda t, c=c, slot=slot: [
                        t.matmul(PS[:, 2 + q, :], ones128[:, :], x2bf[:, slot, 512 * q:512 * (q + 1)],
                                 start=(c == 0), stop=(c == DC - 1)) for q in range(2)][-1],
                        waits=[t_sq], banks=(2, 3) if c == 0 else ())
                    x2slot_guard[slot] = mm_x2_last
                ps_x = PS[:, 0:2, :].rearrange("p a b -> p (a b)")
                ps_x2 = PS[:, 2:4, :].rearrange("p a b -> p (a b)")
                t_mb = S.rd("vector", lambda v, ps_x=ps_x: v.tensor_scalar(
                    MB[:, :], ps_x, c1024, 0.0, Mul, Add),
                    waits=[mm_x_last], banks=(0, 1))
                S.op("vector", lambda v: v.tensor_tensor(t32[:, :], MB[:, :], MB[:, :], Mul),
                     inc="sV")
                t_var = S.rd("vector", lambda v, ps_x2=ps_x2: v.scalar_tensor_tensor(
                    RB[:, :], ps_x2, c1024, t32[:, :], Mul, Sub),
                    waits=[mm_x2_last], banks=(2, 3))
                t_sd = S.op("scalar", lambda s: s.activation(
                    t32[:, :], RB[:, :], AF.Sqrt, bias=epsc, scale=1.0),
                    waits=[t_var], inc="sA")
                t_rb = S.op("vector", lambda v: v.reciprocal(RB[:, :], t32[:, :]),
                            waits=[t_sd], inc="sV")
                for c in range(DC):
                    S.op("vector", lambda v, c=c, h0=h0: v.tensor_tensor(
                        t32[:, :], xT[:, c, h0:h0 + 1024], MB[:, :], Sub),
                        waits=[t_rb], inc="sV")
                    tb = S.op("vector", lambda v, c=c, h0=h0: v.tensor_tensor(
                        lnb[:, c, h0:h0 + 1024], t32[:, :], RB[:, :], Mul), inc="sV",
                        drain=(c == DC - 1) or half == 1)
                    if half == 1:
                        apply_toks[c] = tb
            if not with_amax:
                return apply_toks, None
            am = []
            for c in range(DC):
                am.append(S.op("vector", lambda v, c=c: v.reduce_max(
                    amaxc[:, c:c + 1], lnb[:, c, :], axis=AX, apply_absolute_value=True),
                    waits=[apply_toks[c]], inc="sV", drain=True))
            S.vtiny(lambda v: v.reduce_max(tmp2, amaxc[:, 0:DC], axis=AX), waits=am)
            t_am = S.vtiny(lambda v: v.tensor_scalar(
                vecs[:, 15:16], tmp2, 1.0, 0.0, Mul, Add))
            return apply_toks, t_am

        def ffn_ln(xtoks):
            return layer_norm(xtoks, with_amax=True)

        def ffn_front(xtoks, dbg_base=None, ln_pre=None):
            if ln_pre is None:
                apply_toks, t_am = layer_norm(xtoks, with_amax=True)
            else:
                apply_toks, t_am = ln_pre
            gmax = bounce_allreduce(vecs[:, 15:16], [t_am])
            t_scale = quant_scales(sigx, gmax, sig_tok_x,
                                   dbg_col=dbg_base)
            xq_toks = []
            for c in range(DC):
                i8s = c % 2
                S.op("scalar", lambda s, c=c, i8s=i8s: s.activation(
                    i8b[:, i8s, 0:T], lnb[:, c, :], AF.Identity,
                    bias=0.0, scale=invq),
                    waits=[t_scale, state.get(("i8_guard", i8s))], drain=True)
                ta = S.op("scalar", lambda s, c=c, i8s=i8s: s.activation(
                    lnb[:, c, :], i8b[:, i8s, 0:T], AF.Identity),
                    inc="sA", drain=True)
                state[("i8_guard", i8s)] = ta
                xq_toks.append(ta)
            return apply_toks, t_scale, xq_toks

        def ffn(xtoks, gate_sp, up_sp, down_sp, dbg_base=None, pre=None):
            if pre is None:
                apply_toks, t_scale, xq_toks = ffn_front(xtoks, dbg_base)
            else:
                apply_toks, t_scale, xq_toks = pre
            t_sx = S.vtiny(lambda v: v.tensor_scalar(
                vecs[:, 14:15], sclq, 1.0, 0.0, Mul, Add), waits=[t_scale])
            t_guv = S.op("vector", lambda v: v.tensor_scalar(
                invw[:, 0:44], sgwf[:, 0:44], vecs[:, 14:15], 0.0, Mul, Add),
                waits=[t_sx, state.get("ffn_act_guard")], inc="sV", drain=True)

            hv_toks = [None] * NI
            init_guard = [state.get("wwin_guard"), apply_toks[-1], t_scale]
            win_guard = {0: list(init_guard), 1: list(init_guard)}
            am_toks = []
            for k in range(NI):
                buf = k % 2
                wtok = []
                for w, (wsrc, sp) in enumerate(((wgq2, gate_sp), (wuq2, up_sp))):
                    wtok.append(S.op("sync", lambda s, k=k, w=w, buf=buf, wsrc=wsrc: s.dma_start(
                        out=wwin_gu[:, buf, w, :],
                        in_=wsrc.ap()[k, :, :]),
                        waits=[sp[k]] + win_guard[buf], inc="sL", amt=16))

                last_mm = None
                for half in range(2):
                    b0 = 4 * ((2 * k + half) % 2)
                    banks = (b0, b0 + 1, b0 + 2, b0 + 3)
                    t_mm = S.pe(lambda t, k=k, half=half, b0=b0, buf=buf: [
                        t.matmul(PS[:, b0 + 2 * w + tc, :],
                                 wwin_gu[:, buf, w, 128 * h:128 * (h + 1)],
                                 lnb[:, h, 1024 * half + 512 * tc:1024 * half + 512 * (tc + 1)],
                                 start=(h == 0), stop=(h == DC - 1))
                        for w in range(2) for h in range(DC) for tc in range(2)][-1],
                        waits=list(xq_toks) + wtok, banks=banks)
                    last_mm = t_mm
                    t_sis = []
                    for tc in range(2):
                        t_sis.append(S.rd("scalar", lambda s, k=k, b0=b0, tc=tc: s.activation(
                            sgf[tc][:, :], PS[:, b0 + tc, :], AF.Silu,
                            bias=0.0, scale=invw[:, k:k + 1]),
                            waits=[t_mm, t_guv, state.get(("sg_guard", tc))],
                            banks=(b0 + tc,), drain=(tc == 1)))
                    for tc in range(2):
                        t0 = 1024 * half + 512 * tc
                        t_hv = S.rd("vector", lambda v, k=k, b0=b0, tc=tc, t0=t0: v.scalar_tensor_tensor(
                            hv[k][:, t0:t0 + 512], PS[:, b0 + 2 + tc, :],
                            invw[:, 22 + k:23 + k], sgf[tc][:, :], Mul, Mul),
                            waits=[t_sis[tc], t_guv], banks=(b0 + 2 + tc,))
                        state[("sg_guard", tc)] = t_hv
                        hv_toks[k] = t_hv
                win_guard[buf] = [last_mm]
                am_toks.append(S.op("vector", lambda v, k=k: v.reduce_max(
                    amaxc[:, 8 + k:9 + k], hv[k][:, :], axis=AX, apply_absolute_value=True),
                    waits=[hv_toks[k]], inc="sV", drain=True))

            am = am_toks
            S.vtiny(lambda v: v.reduce_max(tmp2, amaxc[:, 8:8 + NI], axis=AX), waits=am)
            t_am2 = S.vtiny(lambda v: v.tensor_scalar(
                vecs[:, 15:16], tmp2, 1.0, 0.0, Mul, Add))
            gmax2 = bounce_allreduce(vecs[:, 15:16], [t_am2])
            t_scale2 = quant_scales(sigd, gmax2, sig_tok_d,
                                    dbg_col=None if dbg_base is None else dbg_base + 2)
            t_hs = S.vtiny(lambda v: v.tensor_scalar(
                hscl, sclq, 0.5, 0.0, Mul, Add), waits=[t_scale2])
            t_dv = S.op("vector", lambda v: v.tensor_scalar(
                invw[:, 44:52], sgwf[:, 44:52], hscl, 0.0, Mul, Add),
                waits=[t_hs], inc="sV", drain=True)

            hq_toks = []
            for k in range(NI):
                i8s = k % 2
                tq = S.op("vector", lambda v, k=k, i8s=i8s: v.tensor_scalar(
                    i8b[:, i8s, 0:T], hv[k][:, :], invq, 0.0, Mul, Add),
                    waits=[t_scale2, state.get(("i8_guard", i8s))], inc="sV")
                ta = S.op("scalar", lambda s, k=k, i8s=i8s: s.activation(
                    hv[k][:, :], i8b[:, i8s, 0:T], AF.Identity), waits=[tq],
                    inc="sA", drain=True)
                state[("i8_guard", i8s)] = ta
                hq_toks.append(ta)

            xt_done = [None] * DC
            dg0 = win_guard[0] + win_guard[1]
            dwin_guard = {0: dg0, 1: list(dg0)}
            last_mm = None
            for j in range(NJ):
                buf = j % 2
                wt = [S.op("sync", lambda s, j=j, buf=buf: s.dma_start(
                    out=wwin_d[:, buf, :], in_=wdq2.ap()[j, :, :]),
                    waits=[down_sp[j]] + dwin_guard[buf],
                    inc="sL", amt=16)]
                b0 = 4 * (j % 2)
                S.pe(lambda t, j=j, buf=buf, b0=b0: [
                    t.matmul(PS[:, b0 + tc, :],
                             wwin_d[:, buf, 128 * i:128 * (i + 1)],
                             hv[i][:, 512 * tc:512 * (tc + 1)],
                             start=(i == 0), stop=False)
                    for tc in range(NT4) for i in range(6)][-1],
                    waits=hq_toks[0:6] + wt,
                    banks=(b0, b0 + 1, b0 + 2, b0 + 3), drain=False)
                S.pe(lambda t, j=j, buf=buf, b0=b0: [
                    t.matmul(PS[:, b0 + tc, :],
                             wwin_d[:, buf, 128 * i:128 * (i + 1)],
                             hv[i][:, 512 * tc:512 * (tc + 1)],
                             start=False, stop=False)
                    for tc in range(NT4) for i in range(6, 14)][-1],
                    waits=hq_toks[0:14], banks=(), drain=False)
                t_mm = S.pe(lambda t, j=j, buf=buf, b0=b0: [
                    t.matmul(PS[:, b0 + tc, :],
                             wwin_d[:, buf, 128 * i:128 * (i + 1)],
                             hv[i][:, 512 * tc:512 * (tc + 1)],
                             start=False, stop=(i == NI - 1))
                    for tc in range(NT4) for i in range(14, NI)][-1],
                    waits=hq_toks, banks=(), drain=True)
                last_mm = t_mm
                for tc in range(NT4):
                    bank = b0 + tc
                    sg = tc % 2
                    t_st = S.rd("vector", lambda v, j=j, tc=tc, bank=bank, sg=sg: v.scalar_tensor_tensor(
                        sgf[sg][:, :], PS[:, bank, :], invw[:, 44 + j:45 + j],
                        xT[:, j, 512 * tc:512 * (tc + 1)], Mul, Add),
                        waits=[t_mm, t_dv, state.get(("sg_guard", sg))], banks=(bank,))
                    t_cp = S.op("vector", lambda v, j=j, tc=tc, sg=sg: v.tensor_copy(
                        xT[:, j, 512 * tc:512 * (tc + 1)], sgf[sg][:, :]), inc="sV")
                    state[("sg_guard", sg)] = t_cp
                    xt_done[j] = t_cp
                dwin_guard[buf] = [last_mm]
            state["wwin_guard"] = last_mm
            state["ffn_act_guard"] = hq_toks[-1]
            return xt_done

        def attention(xtoks):
            apply_toks, _ = layer_norm(xtoks, with_amax=False)
            lt_toks = []
            for up in range(NTT // 2):
                t_tr = S.pe(lambda t, up=up: [
                    t.transpose(PS[:, 4 + uu, :].bitcast(BF16)[:, 128 * c:128 * (c + 1)],
                                lnb[:, c, 128 * (2 * up + uu):128 * (2 * up + uu) + 128],
                                ident[:, :])
                    for uu in range(2) for c in range(DC)][-1],
                    waits=list(apply_toks) + [t_id], banks=(4, 5))
                for uu in range(2):
                    pb = PS[:, 4 + uu, :].bitcast(BF16)
                    lt_toks.append(S.rd("vector", lambda v, up=up, uu=uu, pb=pb: v.tensor_copy(
                        ln_tok[:, 2 * up + uu, :], pb[:, :]), waits=[t_tr], banks=(4 + uu,),
                        drain=True))

            xt_done = [None] * DC
            ig = [state.get("wwin_guard"), apply_toks[-1]]
            qk_guard = {"init": list(ig)}
            vo_guard = {"init": list(ig)}
            for ch in range(NT4):
                t0 = 512 * ch
                q_toks = []
                mid_tok = None
                for d in range(DC):
                    wslot = d % 4
                    gw = qk_guard["init"] if d < 4 else [mid_tok]
                    t_w = S.op("gpsimd", lambda g, d=d, wslot=wslot: g.dma_start(
                        out=wwin_qk[:, wslot, :], in_=qkw[128 * d:128 * (d + 1), :]),
                        waits=gw, inc="sG", amt=16)
                    t_mm = S.pe(lambda t, d=d, t0=t0, wslot=wslot: [
                        t.matmul(PS[:, e, :], wwin_qk[:, wslot, 128 * e:128 * (e + 1)],
                                 lnb[:, d, t0:t0 + 512],
                                 start=(d == 0), stop=(d == DC - 1))
                        for e in range(DC)][-1],
                        waits=[t_w] + list(apply_toks),
                        banks=tuple(range(8)) if d == 0 else (),
                        drain=(d == 3 or d == DC - 1))
                    if d == 3:
                        mid_tok = t_mm
                    mmq_toks = t_mm
                for e in range(DC):
                    q_toks.append(S.rd("vector", lambda v, e=e: v.tensor_scalar(
                        qc[:, e, :], PS[:, e, :], c32, 0.0, Mul, Add),
                        waits=[mmq_toks], banks=(e,), drain=(e == DC - 1)))
                qk_guard["init"] = [mmq_toks]

                pt_toks = []
                s_toks = {}

                def emit_S(lt):
                    b0 = 4 * (lt % 2)
                    s_toks[lt] = S.pe(lambda t, lt=lt, b0=b0: [
                        t.matmul(PS[:, b0 + uc, :], qc[:, e, 128 * lt:128 * (lt + 1)],
                                 lnb[:, e, 512 * uc:512 * (uc + 1)],
                                 start=(e == 0), stop=(e == DC - 1))
                        for e in range(DC) for uc in range(4)][-1],
                        waits=q_toks, banks=(b0, b0 + 1, b0 + 2, b0 + 3))

                emit_S(0)
                emit_S(1)
                for lt in range(4):
                    b0 = 4 * (lt % 2)
                    banks4 = (b0, b0 + 1, b0 + 2, b0 + 3)
                    ps_s = PS[:, b0:b0 + 4, :].rearrange("p a b -> p (a b)")
                    S.rd("vector", lambda v, ps_s=ps_s: v.reduce_max(tmp1, ps_s, axis=AX),
                         waits=[s_toks[lt], state["tmp_guard"]], banks=banks4, drain=True)
                    t_ng = S.vtiny(lambda v: v.tensor_scalar(
                        tmp2, tmp1, -1.0, 0.0, Mul, Add))
                    state["tmp_guard"] = t_ng
                    t_exp = None
                    for uc in range(4):
                        t_exp = S.rd("scalar", lambda s, uc=uc, b0=b0: s.activation(
                            Praw[:, 512 * uc:512 * (uc + 1)], PS[:, b0 + uc, :], AF.Exp,
                            bias=tmp2, scale=1.0),
                            waits=[t_ng, state.get("praw_guard")], banks=(b0 + uc,))
                    t_sum = S.vtiny(lambda v: v.reduce_sum(
                        tmp1, Praw[:, :], axis=AX), waits=[t_exp])
                    state["tmp_guard"] = t_sum
                    S.vtiny(lambda v: v.reciprocal(tmp2, tmp1))
                    t_pn = S.op("vector", lambda v: v.tensor_scalar(
                        Pn[:, :], Praw[:, :], tmp2, 0.0, Mul, Add),
                        waits=[state.get("pn_guard")], inc="sV")
                    state["praw_guard"] = t_pn
                    tr_banks = (b0, b0 + 1)
                    t_tr = S.pe(lambda t, b0=b0: [
                        t.transpose(PS[:, b0 + (m // 8), :].bitcast(BF16)[:, 128 * (m % 8):128 * (m % 8) + 128],
                                    Pn[:, 128 * m:128 * m + 128], ident[:, :])
                        for m in range(16)][-1],
                        waits=[t_pn, t_id], banks=tr_banks)
                    evs = []
                    for half in range(2):
                        pb = PS[:, b0 + half, :].bitcast(BF16)
                        evs.append(S.rd("vector", lambda v, half=half, lt=lt, pb=pb: v.tensor_copy(
                            PT[:, 8 * half:8 * half + 8, 128 * lt:128 * lt + 128],
                            pb.rearrange("p (a b) -> p a b", a=8)),
                            waits=[t_tr, state.get("pt_guard")], banks=(b0 + half,),
                            drain=(half == 1)))
                    pt_toks += evs
                    state["pn_guard"] = evs[-1]
                    if lt + 2 < 4:
                        emit_S(lt + 2)

                t_mm_av = None
                for d in range(DC):
                    t_mm_av = S.pe(lambda t, d=d: [
                        t.matmul(PS[:, d, :], ln_tok[:, u, 128 * d:128 * (d + 1)],
                                 PT[:, u, :], start=(u == 0), stop=(u == NTT - 1))
                        for u in range(NTT)][-1],
                        waits=pt_toks + lt_toks if d == 0 else [],
                        banks=(d,), drain=(d == DC - 1))
                av_toks = []
                for d in range(DC):
                    av_toks.append(S.rd("vector", lambda v, d=d: v.tensor_copy(
                        avbuf[:, d, :], PS[:, d, :]), waits=[t_mm_av], banks=(d,),
                        drain=(d == DC - 1)))
                state["pt_guard"] = av_toks[-1]

                t_mm_o = None
                vmid = None
                for d in range(DC):
                    wslot = d % 4
                    gw = vo_guard["init"] if d < 4 else [vmid]
                    t_w = S.op("gpsimd", lambda g, d=d, wslot=wslot: g.dma_start(
                        out=wwin_qk[:, wslot, :], in_=vow[128 * d:128 * (d + 1), :]),
                        waits=gw + [av_toks[d]], inc="sG", amt=16)
                    t_mm_o = S.pe(lambda t, d=d, wslot=wslot: [
                        t.matmul(PS[:, e, :], wwin_qk[:, wslot, 128 * e:128 * (e + 1)],
                                 avbuf[:, d, :], start=(d == 0), stop=(d == DC - 1))
                        for e in range(DC)][-1],
                        waits=[t_w] + (av_toks if d == 0 else []),
                        banks=tuple(range(8)) if d == 0 else (),
                        drain=(d == 3 or d == DC - 1))
                    if d == 3:
                        vmid = t_mm_o
                vo_guard["init"] = [t_mm_o]
                qk_guard["init"] = [t_mm_o]
                state["wwin_guard"] = t_mm_o
                for e in range(DC):
                    sg = e % 2
                    t_st = S.rd("vector", lambda v, e=e, t0=t0, sg=sg: v.tensor_tensor(
                        sgf[sg][:, :], PS[:, e, :], xT[:, e, t0:t0 + 512], Add),
                        waits=[t_mm_o, state.get(("sg_guard", sg)), xtoks[e]], banks=(e,))
                    t_cp = S.op("vector", lambda v, e=e, t0=t0, sg=sg: v.tensor_copy(
                        xT[:, e, t0:t0 + 512], sgf[sg][:, :]), inc="sV")
                    state[("sg_guard", sg)] = t_cp
                    if ch == NT4 - 1:
                        xt_done[e] = t_cp
            return xt_done

        dbg_sink = []

        def dump(dst, src_tiles, waits):
            for c, (ap, tok) in enumerate(zip(src_tiles, waits)):
                dbg_sink.append(S.op("gpsimd", lambda g, c=c, ap=ap, dst=dst: g.dma_start(
                    out=dst[128 * c:128 * (c + 1), :], in_=ap), waits=[tok],
                    inc="sG", amt=16))

        ln1 = ffn_ln(t_x)
        gate_spill, up_spill, down_spill = emit_prep()
        pre1 = ffn_front(t_x, dbg_base=0 if debug else None, ln_pre=ln1)
        xt1 = ffn(t_x, gate_spill, up_spill, down_spill, pre=pre1)
        if debug:
            dump(dY1, [xT[:, c, :] for c in range(DC)], xt1)
        xt2 = attention(xt1)
        if debug:
            dump(dY2, [xT[:, c, :] for c in range(DC)], xt2)
        xt3 = ffn(xt2, gate_spill, up_spill, down_spill,
                  dbg_base=4 if debug else None)
        if debug:
            dbg_sink.append(S.op("gpsimd", lambda g: g.dma_start(
                out=dSC[:, :], in_=dbgt[:, :]), waits=[S.tok("sV")], inc="sG", amt=16))
            dbg_sink.append(S.op("gpsimd", lambda g: g.dma_start(
                out=dRM[:, :], in_=rmw[:, :]), waits=[t_inv], inc="sG", amt=16))
            dbg_sink.append(S.op("gpsimd", lambda g: g.dma_start(
                out=dSG[:, :], in_=sgwf[:, :]), waits=[t_inv], inc="sG", amt=16))

        st_toks = [S.op("sync", lambda s, c=c: s.dma_start(
            out=y[128 * c:128 * (c + 1), :], in_=xT[:, c, :]),
            waits=[xt3[c]], inc="sL", amt=16) for c in range(DC)]
        S.op("gpsimd", lambda g: g.nop(), waits=st_toks)

        S.emit(block, sems)
    return nc


_NC_CACHE = None


def kernel(L0_attn_x, L0_attn_QK, L0_attn_VO, L0_ffn_clip_x, L0_ffn_clip_down,
           L0_ffn_w_gate, L0_ffn_w_up, L0_ffn_w_down):
    global _NC_CACHE
    if _NC_CACHE is None:
        _NC_CACHE = build_nc()
    nc = _NC_CACHE
    x = np.asarray(L0_attn_x, dtype=np.float32)
    ins = {
        "wg": np.ascontiguousarray(np.asarray(L0_ffn_w_gate, dtype=np.float32)),
        "wu": np.ascontiguousarray(np.asarray(L0_ffn_w_up, dtype=np.float32)),
        "wd": np.ascontiguousarray(np.asarray(L0_ffn_w_down, dtype=np.float32)),
        "qkw": np.ascontiguousarray(np.asarray(L0_attn_QK, dtype=np.float32)),
        "vow": np.ascontiguousarray(np.asarray(L0_attn_VO, dtype=np.float32)),
        "clx": np.asarray(L0_ffn_clip_x, dtype=np.float32).reshape(1, 1),
        "cld": np.asarray(L0_ffn_clip_down, dtype=np.float32).reshape(1, 1),
    }
    in_maps = [dict(ins, xin=np.ascontiguousarray(x[b].T)) for b in range(NCORE)]
    res = run_bass_kernel_spmd(nc, in_maps, core_ids=list(range(NCORE)))
    out = np.empty((NCORE, T, D), dtype=np.float32)
    for b in range(NCORE):
        out[b] = res.results[b]["y"].T
    return out


# version 33
# speedup vs baseline: 1.0010x; 1.0010x over previous
import contextlib
import numpy as np
import concourse.bass as bass
import concourse.mybir as mybir
from concourse.alu_op_type import AluOpType
from concourse.bass_utils import run_bass_kernel_spmd

F32 = mybir.dt.float32
BF16 = mybir.dt.bfloat16
I8 = mybir.dt.int8
AF = mybir.ActivationFunctionType
AX = mybir.AxisListType.X
Mul = AluOpType.mult
Add = AluOpType.add
Sub = AluOpType.subtract
MaxO = AluOpType.max

NCORE = 8
D = 1024
T = 2048
IFF = 2816
DC = D // 128
NJ = DC
NI = IFF // 128
NTT = T // 128
NT4 = T // 512

ENGINES = ("sync", "gpsimd", "tensor", "vector", "scalar")
OWNED = {"sP": "tensor", "sV": "vector", "sA": "scalar"}


class Sched:
    def __init__(self):
        self.ops = {e: [] for e in ENGINES}
        self.cnt = {s: 0 for s in ("sL", "sS", "sG", "sP", "sV", "sA", "sC", "sW0", "sW1", "sW2", "sW3")}
        self.psum_read = {b: None for b in range(8)}

    def op(self, eng, fn, waits=(), inc=None, amt=1, drain=False, dma=False):
        waits = tuple(w for w in waits if w is not None)
        tok = None
        if inc is not None:
            self.cnt[inc] += amt
            tok = (inc, self.cnt[inc])
            if eng == "scalar" and not dma:
                drain = True
        self.ops[eng].append((fn, waits, inc, amt, drain))
        return tok

    def tok(self, sem):
        return (sem, self.cnt[sem])

    def pe(self, fn, waits=(), banks=(), drain=True):
        w = list(waits) + [self.psum_read[b] for b in banks]
        return self.op("tensor", fn, waits=w, inc="sP", drain=drain)

    def rd(self, eng, fn, waits=(), banks=(), inc=None, drain=False):
        if inc is None:
            inc = {"vector": "sV", "scalar": "sA"}[eng]
        t = self.op(eng, fn, waits=waits, inc=inc, drain=drain)
        for b in banks:
            self.psum_read[b] = t
        return t

    def vtiny(self, fn, waits=(), inc="sV"):
        return self.op("vector", fn, waits=waits, inc=inc, drain=True)

    def emit(self, block, sems):
        def section(eng_name):
            def run(eng):
                waited = {}
                for fn, waits, inc, amt, drain in self.ops[eng_name]:
                    for (s, v) in waits:
                        if OWNED.get(s) == eng_name:
                            continue
                        if waited.get(s, -1) >= v:
                            continue
                        eng.wait_ge(sems[s], v)
                        waited[s] = v
                    ins = fn(eng)
                    if drain:
                        eng.drain()
                    if inc is not None:
                        if ins is not None and not drain:
                            ins.then_inc(sems[inc], amt)
                        else:
                            eng.nop().then_inc(sems[inc], amt)
            return run

        for name in ENGINES:
            if self.ops[name]:
                getattr(block, name)(section(name))


def build_nc(debug=False):
    nc = bass.Bass("TRN2", target_bir_lowering=False)

    xin = nc.dram_tensor("xin", [D, T], F32, kind="ExternalInput")
    wg = nc.dram_tensor("wg", [IFF, D], F32, kind="ExternalInput")
    wu = nc.dram_tensor("wu", [IFF, D], F32, kind="ExternalInput")
    wd = nc.dram_tensor("wd", [D, IFF], F32, kind="ExternalInput")
    qkw = nc.dram_tensor("qkw", [D, D], F32, kind="ExternalInput")
    vow = nc.dram_tensor("vow", [D, D], F32, kind="ExternalInput")
    clx = nc.dram_tensor("clx", [1, 1], F32, kind="ExternalInput")
    cld = nc.dram_tensor("cld", [1, 1], F32, kind="ExternalInput")
    y = nc.dram_tensor("y", [D, T], F32, kind="ExternalOutput")
    wgq2 = nc.dram_tensor("wgq2", [NI, 128, D], BF16)
    wuq2 = nc.dram_tensor("wuq2", [NI, 128, D], BF16)
    wdq2 = nc.dram_tensor("wdq2", [NJ, 128, IFF], BF16)
    cin = nc.dram_tensor("cin", [128, 1], F32)
    cout = nc.dram_tensor("cout", [128, 1], F32)
    gsc = nc.dram_tensor("gsc", [1, 1], F32)
    if debug:
        dY1 = nc.dram_tensor("dY1", [D, T], F32, kind="ExternalOutput")
        dY2 = nc.dram_tensor("dY2", [D, T], F32, kind="ExternalOutput")
        dSC = nc.dram_tensor("dSC", [128, 8], F32, kind="ExternalOutput")
        dRM = nc.dram_tensor("dRM", [128, 80], F32, kind="ExternalOutput")
        dSG = nc.dram_tensor("dSG", [128, 66], F32, kind="ExternalOutput")

    S = Sched()

    with contextlib.ExitStack() as ctx:
        ec = ctx.enter_context
        xT = ec(nc.sbuf_tensor("xT", [128, DC, T], F32))
        lnb = ec(nc.sbuf_tensor("lnb", [128, DC, T], BF16))
        slab = ec(nc.sbuf_tensor("slab", [128, 52736], BF16))
        i8b = ec(nc.sbuf_tensor("i8b", [128, 2, IFF], I8))
        sgw = ec(nc.sbuf_tensor("sgw", [128, 3, 22], F32))
        amaxc = ec(nc.sbuf_tensor("amaxc", [128, 32], F32))
        vecs = ec(nc.sbuf_tensor("vecs", [128, 16], F32))
        ones128 = ec(nc.sbuf_tensor("ones128", [128, 128], BF16))
        ident = ec(nc.sbuf_tensor("ident", [128, 128], BF16))
        crow = ec(nc.sbuf_tensor("crow", [128, 128], F32))
        sml = ec(nc.sbuf_tensor("sml", [128, 16], F32))
        rmw = ec(nc.sbuf_tensor("rmw", [128, 80], F32))
        invw = ec(nc.sbuf_tensor("invw", [128, 64], F32))
        dbgt = ec(nc.sbuf_tensor("dbgt", [128, 8], F32)) if debug else None
        PS = ec(nc.psum_tensor("PS", [128, 8, 512], F32))
        semL = ec(nc.semaphore("sL"))
        semG = ec(nc.semaphore("sG"))
        semP = ec(nc.semaphore("sP"))
        semV = ec(nc.semaphore("sV"))
        semA = ec(nc.semaphore("sA"))
        semC = ec(nc.semaphore("sC"))
        semS = ec(nc.semaphore("sS"))
        semW = [ec(nc.semaphore("sW%d" % i)) for i in range(4)]
        block = ec(nc.Block())
        sems = {"sL": semL, "sG": semG, "sP": semP, "sV": semV, "sA": semA, "sC": semC,
                "sS": semS,
                "sW0": semW[0], "sW1": semW[1], "sW2": semW[2], "sW3": semW[3]}

        def sl(off, n):
            return slab[:, off:off + n]

        hv = [sl(k * T, T) for k in range(NI)]
        ln_tok = sl(0, 16384).rearrange("p (u d) -> p u d", d=1024)
        qc = sl(16384, 4096).rearrange("p (e m) -> p e m", m=512)
        Praw = sl(20480, 2048)
        Pn = sl(22528, 2048)
        PT = sl(24576, 8192).rearrange("p (u m) -> p u m", m=512)
        avbuf = sl(32768, 4096).rearrange("p (d m) -> p d m", m=512)
        xbf = sl(36864, 4096).rearrange("p (s m) -> p s m", s=4)
        x2bf = sl(40960, 4096).rearrange("p (s m) -> p s m", s=4)
        wwin_gu = sl(45056, 4096).rearrange("p (b w m) -> p b w m", b=2, w=2)
        wwin_d = sl(45056, 5632).rearrange("p (b m) -> p b m", b=2)
        wwin_qk = sl(45056, 4096).rearrange("p (b m) -> p b m", b=4)
        MB = sl(45056, 2048).bitcast(F32)
        RB = sl(47104, 2048).bitcast(F32)
        t32 = sl(49152, 2048).bitcast(F32)
        sgf2 = sl(50688, 2048).bitcast(F32)
        sgf = [sgf2[:, 0:512], sgf2[:, 512:1024]]

        dbg = sml
        c1024 = sml[:, 1:2]
        epsc = sml[:, 2:3]
        sigx = sml[:, 3:4]
        sigd = sml[:, 4:5]
        invq = sml[:, 5:6]
        sclq = sml[:, 6:7]
        c32 = sml[:, 7:8]
        hscl = sml[:, 8:9]
        tmp1 = sml[:, 9:10]
        tmp2 = sml[:, 10:11]

        S.op("vector", lambda v: v.memset(sml[:, :], 0.0))
        S.op("vector", lambda v: v.memset(c1024, 1.0 / 1024.0))
        S.op("vector", lambda v: v.memset(epsc, 1e-5))
        S.op("vector", lambda v: v.memset(c32, 1.0 / 32.0))
        S.op("vector", lambda v: v.memset(ones128[:, :], 1.0))
        S.op("gpsimd", lambda g: g.memset(ident[:, :], 0.0))
        t_id = S.op("gpsimd", lambda g: g.affine_select(
            out=ident[:, :], in_=ident[:, :], compare_op=AluOpType.not_equal,
            fill=1.0, base=0, pattern=[[-1, 128]], channel_multiplier=1), inc="sC")

        t_cl = S.op("gpsimd", lambda g: g.dma_start(
            out=sml[:, 11:12], in_=bass.AP(clx, 0, [[0, 128], [1, 1]])), inc="sG", amt=16)
        t_cl2 = S.op("gpsimd", lambda g: g.dma_start(
            out=sml[:, 12:13], in_=bass.AP(cld, 0, [[0, 128], [1, 1]])), inc="sG", amt=16)
        sig_tok_x = S.op("scalar", lambda s: s.activation(sigx, sml[:, 11:12], AF.Sigmoid),
                         waits=[t_cl], inc="sA", drain=True)
        sig_tok_d = S.op("scalar", lambda s: s.activation(sigd, sml[:, 12:13], AF.Sigmoid),
                         waits=[t_cl2], inc="sA", drain=True)

        t_x = [S.op("scalar", lambda s, c=c: s.dma_start(
            out=xT[:, c, :], in_=xin[128 * c:128 * (c + 1), :]), inc="sS", amt=16,
            dma=True) for c in range(DC)]

        state = {"tmp_guard": None, "coll_guard": None, "ffn_act_guard": None}
        prep_ctr = [0]
        prep_slot_guard = {}
        sgwf = sgw.ap().rearrange("p a b -> p (a b)")

        def emit_prep():
            pk = rmw
            pk2 = invw
            prep_bank = [0]
            slot_guard = {}

            def quant_tail(n, ld, cols, mult_ap, dst_ap, sc_tok):
                slot = n % 8
                base = 4096 * slot
                qb = sl(base + 2048, 1024)
                tb = sl(base + 3072, 1024)
                g = slot_guard.get(slot, (None, None, None))
                i8s = n % 2
                S.op("scalar", lambda s, ld=ld, cols=cols, i8s=i8s, mult_ap=mult_ap: s.activation(
                    i8b[:, i8s, 0:cols], ld[:, 0:cols], AF.Identity,
                    bias=0.0, scale=mult_ap),
                    waits=[sc_tok, state.get(("i8_guard", i8s))], drain=True)
                t_b = S.op("scalar", lambda s, qb=qb, cols=cols, i8s=i8s: s.activation(
                    qb[:, 0:cols], i8b[:, i8s, 0:cols], AF.Identity),
                    waits=[g[1]], inc="sA", drain=True)
                state[("i8_guard", i8s)] = t_b
                bank = 4 + (prep_bank[0] % 4)
                prep_bank[0] += 1
                pb = PS[:, bank, :].bitcast(BF16)
                t_tr = S.pe(lambda t, qb=qb, pb=pb, nblk=cols // 128: [
                    t.transpose(pb[:, 128 * m:128 * (m + 1)],
                                qb[:, 128 * m:128 * (m + 1)], ident[:, :])
                    for m in range(cols // 128)][-1],
                    waits=[t_b, t_id], banks=(bank,))
                t_ev = S.rd("vector", lambda v, tb=tb, cols=cols, pb=pb: v.tensor_copy(
                    tb[:, 0:cols], pb[:, 0:cols]),
                    waits=[t_tr, g[2]], banks=(bank,), drain=True)
                t_s = S.op("sync", lambda sy, dst_ap=dst_ap, tb=tb, cols=cols: sy.dma_start(
                    out=dst_ap, in_=tb[:, 0:cols]), waits=[t_ev], inc="sL", amt=16)
                slot_guard[slot] = (t_s, t_tr, t_s)
                return t_s

            def load(n, src_ap, cols):
                slot = n % 8
                ld = sl(4096 * slot, 2048).bitcast(F32)
                g = slot_guard.get(slot, (None, None, None))
                t_l = S.op("gpsimd", lambda gp, src_ap=src_ap, ld=ld, cols=cols: gp.dma_start(
                    out=ld[:, 0:cols], in_=src_ap), waits=[g[0]],
                    inc="sW%d" % (n % 4), amt=16, dma=True)
                return ld, t_l

            gate_spill = [None] * NI
            up_spill = [None] * NI
            n = 0
            for kp in range(11):
                items = []
                for kk in range(2):
                    k = 2 * kp + kk
                    for widx, wsrc in ((0, wg), (1, wu)):
                        col = 2 * k + widx
                        ld, t_l = load(n + len(items), wsrc[128 * k:128 * (k + 1), :], 1024)
                        t_r = S.op("vector", lambda v, ld=ld, col=col: v.reduce_max(
                            pk[:, col:col + 1], ld[:, 0:1024], axis=AX,
                            apply_absolute_value=True),
                            waits=[t_l], inc="sV", drain=(len(items) == 3))
                        items.append((ld, col, t_r))
                c0 = 4 * kp
                S.op("vector", lambda v, c0=c0: v.reciprocal(
                    pk2[:, c0:c0 + 4], pk[:, c0:c0 + 4]),
                    waits=[items[-1][2]], inc="sV", drain=True)
                sc_tok = S.op("vector", lambda v, c0=c0: v.tensor_scalar(
                    pk2[:, c0:c0 + 4], pk2[:, c0:c0 + 4], 127.0, 0.0, Mul, Add),
                    inc="sV", drain=True)
                for idx, (ld, col, _) in enumerate(items):
                    k = 2 * kp + idx // 2
                    widx = idx % 2
                    dst = (wgq2 if widx == 0 else wuq2).ap()[k, :, :]
                    t_s = quant_tail(n + idx, ld, 1024, pk2[:, col:col + 1], dst, sc_tok)
                    if widx == 0:
                        gate_spill[k] = t_s
                    else:
                        up_spill[k] = t_s
                n += 4

            down_spill = []
            for j in range(NJ):
                items = []
                for cc in range(3):
                    w = min(1024, IFF - 1024 * cc)
                    col = 44 + 3 * j + cc
                    ld, t_l = load(n + cc, wd[128 * j:128 * (j + 1), 1024 * cc:1024 * cc + w], w)
                    t_r = S.op("vector", lambda v, ld=ld, col=col, w=w: v.reduce_max(
                        pk[:, col:col + 1], ld[:, 0:w], axis=AX,
                        apply_absolute_value=True),
                        waits=[t_l], inc="sV", drain=(cc == 2))
                    items.append((ld, w, t_r))
                S.vtiny(lambda v, j=j: v.reduce_max(
                    pk[:, 68 + j:69 + j], pk[:, 44 + 3 * j:47 + 3 * j], axis=AX),
                    waits=[items[-1][2]])
                S.op("vector", lambda v, j=j: v.reciprocal(
                    pk2[:, 44 + j:45 + j], pk[:, 68 + j:69 + j]), inc="sV", drain=True)
                sc_tok = S.op("vector", lambda v, j=j: v.tensor_scalar(
                    pk2[:, 44 + j:45 + j], pk2[:, 44 + j:45 + j], 127.0, 0.0, Mul, Add),
                    inc="sV", drain=True)
                last = None
                for cc, (ld, w, _) in enumerate(items):
                    last = quant_tail(n + cc, ld, w, pk2[:, 44 + j:45 + j],
                                      wdq2.ap()[j, :, 1024 * cc:1024 * cc + w], sc_tok)
                n += 3
                down_spill.append(last)
            gca = bass.AP(pk[:, :].tensor, 0, [[80, 128], [2, 22]])
            uca = bass.AP(pk[:, :].tensor, 1, [[80, 128], [2, 22]])
            S.op("vector", lambda v, gca=gca: v.tensor_scalar(
                sgwf[:, 0:22], gca, 1.0 / 127.0, 0.0, Mul, Add), inc="sV", drain=True)
            S.op("vector", lambda v, uca=uca: v.tensor_scalar(
                sgwf[:, 22:44], uca, 1.0 / 127.0, 0.0, Mul, Add), inc="sV", drain=True)
            S.op("vector", lambda v: v.tensor_scalar(
                sgwf[:, 44:52], pk[:, 68:76], 1.0 / 127.0, 0.0, Mul, Add),
                inc="sV", drain=True)
            return gate_spill, up_spill, down_spill


        def bounce_allreduce(src_ap, waits):
            t1 = S.op("gpsimd", lambda g: g.dma_start(out=cin[:, :], in_=src_ap),
                      waits=list(waits) + [state["coll_guard"]], inc="sG", amt=16)
            t2 = S.op("gpsimd", lambda g: g.collective_compute(
                "AllReduce", MaxO, replica_groups=[list(range(NCORE))],
                ins=[cin.ap()], outs=[cout.ap()]), waits=[t1], inc="sC")
            state["coll_guard"] = t2
            t3 = S.op("gpsimd", lambda g: g.dma_start(
                out=crow[:, :], in_=bass.AP(cout, 0, [[0, 128], [1, 128]])),
                waits=[t2, state["tmp_guard"]], inc="sG", amt=16)
            t4 = S.vtiny(lambda v: v.reduce_max(tmp1, crow[:, :], axis=AX),
                         waits=[t3])
            return t4

        def quant_scales(sig_ap, gmax_tok, sig_tok, dbg_col=None):
            t1 = S.vtiny(lambda v, sig_ap=sig_ap: v.tensor_tensor(
                tmp2, tmp1, sig_ap, Mul), waits=[gmax_tok, sig_tok])
            state["tmp_guard"] = t1
            S.vtiny(lambda v: v.tensor_scalar(sclq, tmp2, 1.0 / 127.0, 0.0, Mul, Add))
            t3 = S.vtiny(lambda v: v.reciprocal(invq, sclq))
            if debug and dbg_col is not None:
                S.op("vector", lambda v, c=dbg_col: v.tensor_copy(
                    dbgt[:, c:c + 1], sclq), inc="sV")
                S.op("vector", lambda v, c=dbg_col: v.tensor_copy(
                    dbgt[:, c + 1:c + 2], tmp1), inc="sV")
            return t3

        def layer_norm(xtoks, with_amax):
            apply_toks = [None] * DC
            xslot_guard = {}
            x2slot_guard = {}
            for half in range(2):
                h0 = 1024 * half
                mm_x_last = mm_x2_last = None
                for c in range(DC):
                    slot = c % 4
                    t_cast = S.op("vector", lambda v, c=c, slot=slot, h0=h0: v.tensor_copy(
                        xbf[:, slot, :], xT[:, c, h0:h0 + 1024]),
                        waits=list(xtoks) + [xslot_guard.get(slot)], inc="sV", drain=True)
                    t_sq = S.op("scalar", lambda s, c=c, slot=slot, h0=h0: s.activation(
                        x2bf[:, slot, :], xT[:, c, h0:h0 + 1024], AF.Square),
                        waits=list(xtoks) + [x2slot_guard.get(slot)], inc="sA")
                    mm_x_last = S.pe(lambda t, c=c, slot=slot: [
                        t.matmul(PS[:, q, :], ones128[:, :], xbf[:, slot, 512 * q:512 * (q + 1)],
                                 start=(c == 0), stop=(c == DC - 1)) for q in range(2)][-1],
                        waits=[t_cast], banks=(0, 1) if c == 0 else ())
                    xslot_guard[slot] = mm_x_last
                    mm_x2_last = S.pe(lambda t, c=c, slot=slot: [
                        t.matmul(PS[:, 2 + q, :], ones128[:, :], x2bf[:, slot, 512 * q:512 * (q + 1)],
                                 start=(c == 0), stop=(c == DC - 1)) for q in range(2)][-1],
                        waits=[t_sq], banks=(2, 3) if c == 0 else ())
                    x2slot_guard[slot] = mm_x2_last
                ps_x = PS[:, 0:2, :].rearrange("p a b -> p (a b)")
                ps_x2 = PS[:, 2:4, :].rearrange("p a b -> p (a b)")
                t_mb = S.rd("vector", lambda v, ps_x=ps_x: v.tensor_scalar(
                    MB[:, :], ps_x, c1024, 0.0, Mul, Add),
                    waits=[mm_x_last], banks=(0, 1))
                S.op("vector", lambda v: v.tensor_tensor(t32[:, :], MB[:, :], MB[:, :], Mul),
                     inc="sV")
                t_var = S.rd("vector", lambda v, ps_x2=ps_x2: v.scalar_tensor_tensor(
                    RB[:, :], ps_x2, c1024, t32[:, :], Mul, Sub),
                    waits=[mm_x2_last], banks=(2, 3))
                t_sd = S.op("scalar", lambda s: s.activation(
                    t32[:, :], RB[:, :], AF.Sqrt, bias=epsc, scale=1.0),
                    waits=[t_var], inc="sA")
                t_rb = S.op("vector", lambda v: v.reciprocal(RB[:, :], t32[:, :]),
                            waits=[t_sd], inc="sV")
                for c in range(DC):
                    S.op("vector", lambda v, c=c, h0=h0: v.tensor_tensor(
                        t32[:, :], xT[:, c, h0:h0 + 1024], MB[:, :], Sub),
                        waits=[t_rb], inc="sV")
                    tb = S.op("vector", lambda v, c=c, h0=h0: v.tensor_tensor(
                        lnb[:, c, h0:h0 + 1024], t32[:, :], RB[:, :], Mul), inc="sV",
                        drain=(c == DC - 1) or half == 1)
                    if half == 1:
                        apply_toks[c] = tb
            if not with_amax:
                return apply_toks, None
            am = []
            for c in range(DC):
                am.append(S.op("vector", lambda v, c=c: v.reduce_max(
                    amaxc[:, c:c + 1], lnb[:, c, :], axis=AX, apply_absolute_value=True),
                    waits=[apply_toks[c]], inc="sV", drain=True))
            S.vtiny(lambda v: v.reduce_max(tmp2, amaxc[:, 0:DC], axis=AX), waits=am)
            t_am = S.vtiny(lambda v: v.tensor_scalar(
                vecs[:, 15:16], tmp2, 1.0, 0.0, Mul, Add))
            return apply_toks, t_am

        def ffn_ln(xtoks):
            return layer_norm(xtoks, with_amax=True)

        def ffn_front(xtoks, dbg_base=None, ln_pre=None):
            if ln_pre is None:
                apply_toks, t_am = layer_norm(xtoks, with_amax=True)
            else:
                apply_toks, t_am = ln_pre
            gmax = bounce_allreduce(vecs[:, 15:16], [t_am])
            t_scale = quant_scales(sigx, gmax, sig_tok_x,
                                   dbg_col=dbg_base)
            xq_toks = []
            for c in range(DC):
                i8s = c % 2
                S.op("scalar", lambda s, c=c, i8s=i8s: s.activation(
                    i8b[:, i8s, 0:T], lnb[:, c, :], AF.Identity,
                    bias=0.0, scale=invq),
                    waits=[t_scale, state.get(("i8_guard", i8s))], drain=True)
                ta = S.op("scalar", lambda s, c=c, i8s=i8s: s.activation(
                    lnb[:, c, :], i8b[:, i8s, 0:T], AF.Identity),
                    inc="sA", drain=True)
                state[("i8_guard", i8s)] = ta
                xq_toks.append(ta)
            return apply_toks, t_scale, xq_toks

        def ffn(xtoks, gate_sp, up_sp, down_sp, dbg_base=None, pre=None):
            if pre is None:
                apply_toks, t_scale, xq_toks = ffn_front(xtoks, dbg_base)
            else:
                apply_toks, t_scale, xq_toks = pre
            t_sx = S.vtiny(lambda v: v.tensor_scalar(
                vecs[:, 14:15], sclq, 1.0, 0.0, Mul, Add), waits=[t_scale])
            t_guv = S.op("vector", lambda v: v.tensor_scalar(
                invw[:, 0:44], sgwf[:, 0:44], vecs[:, 14:15], 0.0, Mul, Add),
                waits=[t_sx, state.get("ffn_act_guard")], inc="sV", drain=True)

            hv_toks = [None] * NI
            init_guard = [state.get("wwin_guard"), apply_toks[-1], t_scale]
            win_guard = {0: list(init_guard), 1: list(init_guard)}
            am_toks = []
            for k in range(NI):
                buf = k % 2
                wtok = []
                for w, (wsrc, sp) in enumerate(((wgq2, gate_sp), (wuq2, up_sp))):
                    wtok.append(S.op("sync", lambda s, k=k, w=w, buf=buf, wsrc=wsrc: s.dma_start(
                        out=wwin_gu[:, buf, w, :],
                        in_=wsrc.ap()[k, :, :]),
                        waits=[sp[k]] + win_guard[buf], inc="sL", amt=16))

                last_mm = None
                for half in range(2):
                    b0 = 4 * ((2 * k + half) % 2)
                    banks = (b0, b0 + 1, b0 + 2, b0 + 3)
                    t_mm = S.pe(lambda t, k=k, half=half, b0=b0, buf=buf: [
                        t.matmul(PS[:, b0 + 2 * w + tc, :],
                                 wwin_gu[:, buf, w, 128 * h:128 * (h + 1)],
                                 lnb[:, h, 1024 * half + 512 * tc:1024 * half + 512 * (tc + 1)],
                                 start=(h == 0), stop=(h == DC - 1))
                        for w in range(2) for h in range(DC) for tc in range(2)][-1],
                        waits=list(xq_toks) + wtok, banks=banks)
                    last_mm = t_mm
                    t_sis = []
                    for tc in range(2):
                        t_sis.append(S.rd("scalar", lambda s, k=k, b0=b0, tc=tc: s.activation(
                            sgf[tc][:, :], PS[:, b0 + tc, :], AF.Silu,
                            bias=0.0, scale=invw[:, k:k + 1]),
                            waits=[t_mm, t_guv, state.get(("sg_guard", tc))],
                            banks=(b0 + tc,), drain=(tc == 1)))
                    for tc in range(2):
                        t0 = 1024 * half + 512 * tc
                        t_hv = S.rd("vector", lambda v, k=k, b0=b0, tc=tc, t0=t0: v.scalar_tensor_tensor(
                            hv[k][:, t0:t0 + 512], PS[:, b0 + 2 + tc, :],
                            invw[:, 22 + k:23 + k], sgf[tc][:, :], Mul, Mul),
                            waits=[t_sis[tc], t_guv], banks=(b0 + 2 + tc,))
                        state[("sg_guard", tc)] = t_hv
                        hv_toks[k] = t_hv
                win_guard[buf] = [last_mm]
                am_toks.append(S.op("vector", lambda v, k=k: v.reduce_max(
                    amaxc[:, 8 + k:9 + k], hv[k][:, :], axis=AX, apply_absolute_value=True),
                    waits=[hv_toks[k]], inc="sV", drain=True))

            am = am_toks
            S.vtiny(lambda v: v.reduce_max(tmp2, amaxc[:, 8:8 + NI], axis=AX), waits=am)
            t_am2 = S.vtiny(lambda v: v.tensor_scalar(
                vecs[:, 15:16], tmp2, 1.0, 0.0, Mul, Add))
            gmax2 = bounce_allreduce(vecs[:, 15:16], [t_am2])
            t_scale2 = quant_scales(sigd, gmax2, sig_tok_d,
                                    dbg_col=None if dbg_base is None else dbg_base + 2)
            t_hs = S.vtiny(lambda v: v.tensor_scalar(
                hscl, sclq, 0.5, 0.0, Mul, Add), waits=[t_scale2])
            t_dv = S.op("vector", lambda v: v.tensor_scalar(
                invw[:, 44:52], sgwf[:, 44:52], hscl, 0.0, Mul, Add),
                waits=[t_hs], inc="sV", drain=True)

            hq_toks = []
            for k in range(NI):
                i8s = k % 2
                tq = S.op("vector", lambda v, k=k, i8s=i8s: v.tensor_scalar(
                    i8b[:, i8s, 0:T], hv[k][:, :], invq, 0.0, Mul, Add),
                    waits=[t_scale2, state.get(("i8_guard", i8s))], inc="sV")
                ta = S.op("scalar", lambda s, k=k, i8s=i8s: s.activation(
                    hv[k][:, :], i8b[:, i8s, 0:T], AF.Identity), waits=[tq],
                    inc="sA", drain=True)
                state[("i8_guard", i8s)] = ta
                hq_toks.append(ta)

            xt_done = [None] * DC
            dg0 = win_guard[0] + win_guard[1]
            dwin_guard = {0: dg0, 1: list(dg0)}
            last_mm = None
            for j in range(NJ):
                buf = j % 2
                wt = [S.op("sync", lambda s, j=j, buf=buf: s.dma_start(
                    out=wwin_d[:, buf, :], in_=wdq2.ap()[j, :, :]),
                    waits=[down_sp[j]] + dwin_guard[buf],
                    inc="sL", amt=16)]
                b0 = 4 * (j % 2)
                S.pe(lambda t, j=j, buf=buf, b0=b0: [
                    t.matmul(PS[:, b0 + tc, :],
                             wwin_d[:, buf, 128 * i:128 * (i + 1)],
                             hv[i][:, 512 * tc:512 * (tc + 1)],
                             start=(i == 0), stop=False)
                    for tc in range(NT4) for i in range(11)][-1],
                    waits=hq_toks[0:11] + wt,
                    banks=(b0, b0 + 1, b0 + 2, b0 + 3), drain=False)
                t_mm = S.pe(lambda t, j=j, buf=buf, b0=b0: [
                    t.matmul(PS[:, b0 + tc, :],
                             wwin_d[:, buf, 128 * i:128 * (i + 1)],
                             hv[i][:, 512 * tc:512 * (tc + 1)],
                             start=False, stop=(i == NI - 1))
                    for tc in range(NT4) for i in range(11, NI)][-1],
                    waits=hq_toks, banks=(), drain=True)
                last_mm = t_mm
                for tc in range(NT4):
                    bank = b0 + tc
                    sg = tc % 2
                    t_st = S.rd("vector", lambda v, j=j, tc=tc, bank=bank, sg=sg: v.scalar_tensor_tensor(
                        sgf[sg][:, :], PS[:, bank, :], invw[:, 44 + j:45 + j],
                        xT[:, j, 512 * tc:512 * (tc + 1)], Mul, Add),
                        waits=[t_mm, t_dv, state.get(("sg_guard", sg))], banks=(bank,))
                    t_cp = S.op("vector", lambda v, j=j, tc=tc, sg=sg: v.tensor_copy(
                        xT[:, j, 512 * tc:512 * (tc + 1)], sgf[sg][:, :]), inc="sV")
                    state[("sg_guard", sg)] = t_cp
                    xt_done[j] = t_cp
                dwin_guard[buf] = [last_mm]
            state["wwin_guard"] = last_mm
            state["ffn_act_guard"] = hq_toks[-1]
            return xt_done

        def attention(xtoks):
            apply_toks, _ = layer_norm(xtoks, with_amax=False)
            lt_toks = []
            for up in range(NTT // 2):
                t_tr = S.pe(lambda t, up=up: [
                    t.transpose(PS[:, 4 + uu, :].bitcast(BF16)[:, 128 * c:128 * (c + 1)],
                                lnb[:, c, 128 * (2 * up + uu):128 * (2 * up + uu) + 128],
                                ident[:, :])
                    for uu in range(2) for c in range(DC)][-1],
                    waits=list(apply_toks) + [t_id], banks=(4, 5))
                for uu in range(2):
                    pb = PS[:, 4 + uu, :].bitcast(BF16)
                    lt_toks.append(S.rd("vector", lambda v, up=up, uu=uu, pb=pb: v.tensor_copy(
                        ln_tok[:, 2 * up + uu, :], pb[:, :]), waits=[t_tr], banks=(4 + uu,),
                        drain=True))

            xt_done = [None] * DC
            ig = [state.get("wwin_guard"), apply_toks[-1]]
            qk_guard = {"init": list(ig)}
            vo_guard = {"init": list(ig)}
            for ch in range(NT4):
                t0 = 512 * ch
                q_toks = []
                mid_tok = None
                for d in range(DC):
                    wslot = d % 4
                    gw = qk_guard["init"] if d < 4 else [mid_tok]
                    t_w = S.op("gpsimd", lambda g, d=d, wslot=wslot: g.dma_start(
                        out=wwin_qk[:, wslot, :], in_=qkw[128 * d:128 * (d + 1), :]),
                        waits=gw, inc="sG", amt=16)
                    t_mm = S.pe(lambda t, d=d, t0=t0, wslot=wslot: [
                        t.matmul(PS[:, e, :], wwin_qk[:, wslot, 128 * e:128 * (e + 1)],
                                 lnb[:, d, t0:t0 + 512],
                                 start=(d == 0), stop=(d == DC - 1))
                        for e in range(DC)][-1],
                        waits=[t_w] + list(apply_toks),
                        banks=tuple(range(8)) if d == 0 else (),
                        drain=(d == 3 or d == DC - 1))
                    if d == 3:
                        mid_tok = t_mm
                    mmq_toks = t_mm
                for e in range(DC):
                    q_toks.append(S.rd("vector", lambda v, e=e: v.tensor_scalar(
                        qc[:, e, :], PS[:, e, :], c32, 0.0, Mul, Add),
                        waits=[mmq_toks], banks=(e,), drain=(e == DC - 1)))
                qk_guard["init"] = [mmq_toks]

                pt_toks = []
                s_toks = {}

                def emit_S(lt):
                    b0 = 4 * (lt % 2)
                    s_toks[lt] = S.pe(lambda t, lt=lt, b0=b0: [
                        t.matmul(PS[:, b0 + uc, :], qc[:, e, 128 * lt:128 * (lt + 1)],
                                 lnb[:, e, 512 * uc:512 * (uc + 1)],
                                 start=(e == 0), stop=(e == DC - 1))
                        for e in range(DC) for uc in range(4)][-1],
                        waits=q_toks, banks=(b0, b0 + 1, b0 + 2, b0 + 3))

                emit_S(0)
                emit_S(1)
                for lt in range(4):
                    b0 = 4 * (lt % 2)
                    banks4 = (b0, b0 + 1, b0 + 2, b0 + 3)
                    ps_s = PS[:, b0:b0 + 4, :].rearrange("p a b -> p (a b)")
                    S.rd("vector", lambda v, ps_s=ps_s: v.reduce_max(tmp1, ps_s, axis=AX),
                         waits=[s_toks[lt], state["tmp_guard"]], banks=banks4, drain=True)
                    t_ng = S.vtiny(lambda v: v.tensor_scalar(
                        tmp2, tmp1, -1.0, 0.0, Mul, Add))
                    state["tmp_guard"] = t_ng
                    t_exp = None
                    for uc in range(4):
                        t_exp = S.rd("scalar", lambda s, uc=uc, b0=b0: s.activation(
                            Praw[:, 512 * uc:512 * (uc + 1)], PS[:, b0 + uc, :], AF.Exp,
                            bias=tmp2, scale=1.0),
                            waits=[t_ng, state.get("praw_guard")], banks=(b0 + uc,))
                    t_sum = S.vtiny(lambda v: v.reduce_sum(
                        tmp1, Praw[:, :], axis=AX), waits=[t_exp])
                    state["tmp_guard"] = t_sum
                    S.vtiny(lambda v: v.reciprocal(tmp2, tmp1))
                    t_pn = S.op("vector", lambda v: v.tensor_scalar(
                        Pn[:, :], Praw[:, :], tmp2, 0.0, Mul, Add),
                        waits=[state.get("pn_guard")], inc="sV")
                    state["praw_guard"] = t_pn
                    tr_banks = (b0, b0 + 1)
                    t_tr = S.pe(lambda t, b0=b0: [
                        t.transpose(PS[:, b0 + (m // 8), :].bitcast(BF16)[:, 128 * (m % 8):128 * (m % 8) + 128],
                                    Pn[:, 128 * m:128 * m + 128], ident[:, :])
                        for m in range(16)][-1],
                        waits=[t_pn, t_id], banks=tr_banks)
                    evs = []
                    for half in range(2):
                        pb = PS[:, b0 + half, :].bitcast(BF16)
                        evs.append(S.rd("vector", lambda v, half=half, lt=lt, pb=pb: v.tensor_copy(
                            PT[:, 8 * half:8 * half + 8, 128 * lt:128 * lt + 128],
                            pb.rearrange("p (a b) -> p a b", a=8)),
                            waits=[t_tr, state.get("pt_guard")], banks=(b0 + half,),
                            drain=(half == 1)))
                    pt_toks += evs
                    state["pn_guard"] = evs[-1]
                    if lt + 2 < 4:
                        emit_S(lt + 2)

                t_mm_av = None
                for d in range(DC):
                    t_mm_av = S.pe(lambda t, d=d: [
                        t.matmul(PS[:, d, :], ln_tok[:, u, 128 * d:128 * (d + 1)],
                                 PT[:, u, :], start=(u == 0), stop=(u == NTT - 1))
                        for u in range(NTT)][-1],
                        waits=pt_toks + lt_toks if d == 0 else [],
                        banks=(d,), drain=(d == DC - 1))
                av_toks = []
                for d in range(DC):
                    av_toks.append(S.rd("vector", lambda v, d=d: v.tensor_copy(
                        avbuf[:, d, :], PS[:, d, :]), waits=[t_mm_av], banks=(d,),
                        drain=(d == DC - 1)))
                state["pt_guard"] = av_toks[-1]

                t_mm_o = None
                vmid = None
                for d in range(DC):
                    wslot = d % 4
                    gw = vo_guard["init"] if d < 4 else [vmid]
                    t_w = S.op("gpsimd", lambda g, d=d, wslot=wslot: g.dma_start(
                        out=wwin_qk[:, wslot, :], in_=vow[128 * d:128 * (d + 1), :]),
                        waits=gw + [av_toks[d]], inc="sG", amt=16)
                    t_mm_o = S.pe(lambda t, d=d, wslot=wslot: [
                        t.matmul(PS[:, e, :], wwin_qk[:, wslot, 128 * e:128 * (e + 1)],
                                 avbuf[:, d, :], start=(d == 0), stop=(d == DC - 1))
                        for e in range(DC)][-1],
                        waits=[t_w] + (av_toks if d == 0 else []),
                        banks=tuple(range(8)) if d == 0 else (),
                        drain=(d == 3 or d == DC - 1))
                    if d == 3:
                        vmid = t_mm_o
                vo_guard["init"] = [t_mm_o]
                qk_guard["init"] = [t_mm_o]
                state["wwin_guard"] = t_mm_o
                for e in range(DC):
                    sg = e % 2
                    t_st = S.rd("vector", lambda v, e=e, t0=t0, sg=sg: v.tensor_tensor(
                        sgf[sg][:, :], PS[:, e, :], xT[:, e, t0:t0 + 512], Add),
                        waits=[t_mm_o, state.get(("sg_guard", sg)), xtoks[e]], banks=(e,))
                    t_cp = S.op("vector", lambda v, e=e, t0=t0, sg=sg: v.tensor_copy(
                        xT[:, e, t0:t0 + 512], sgf[sg][:, :]), inc="sV")
                    state[("sg_guard", sg)] = t_cp
                    if ch == NT4 - 1:
                        xt_done[e] = t_cp
            return xt_done

        dbg_sink = []

        def dump(dst, src_tiles, waits):
            for c, (ap, tok) in enumerate(zip(src_tiles, waits)):
                dbg_sink.append(S.op("gpsimd", lambda g, c=c, ap=ap, dst=dst: g.dma_start(
                    out=dst[128 * c:128 * (c + 1), :], in_=ap), waits=[tok],
                    inc="sG", amt=16))

        ln1 = ffn_ln(t_x)
        gate_spill, up_spill, down_spill = emit_prep()
        pre1 = ffn_front(t_x, dbg_base=0 if debug else None, ln_pre=ln1)
        xt1 = ffn(t_x, gate_spill, up_spill, down_spill, pre=pre1)
        if debug:
            dump(dY1, [xT[:, c, :] for c in range(DC)], xt1)
        xt2 = attention(xt1)
        if debug:
            dump(dY2, [xT[:, c, :] for c in range(DC)], xt2)
        xt3 = ffn(xt2, gate_spill, up_spill, down_spill,
                  dbg_base=4 if debug else None)
        if debug:
            dbg_sink.append(S.op("gpsimd", lambda g: g.dma_start(
                out=dSC[:, :], in_=dbgt[:, :]), waits=[S.tok("sV")], inc="sG", amt=16))
            dbg_sink.append(S.op("gpsimd", lambda g: g.dma_start(
                out=dRM[:, :], in_=rmw[:, :]), waits=[t_inv], inc="sG", amt=16))
            dbg_sink.append(S.op("gpsimd", lambda g: g.dma_start(
                out=dSG[:, :], in_=sgwf[:, :]), waits=[t_inv], inc="sG", amt=16))

        st_toks = [S.op("sync", lambda s, c=c: s.dma_start(
            out=y[128 * c:128 * (c + 1), :], in_=xT[:, c, :]),
            waits=[xt3[c]], inc="sL", amt=16) for c in range(DC)]
        S.op("gpsimd", lambda g: g.nop(), waits=st_toks)

        S.emit(block, sems)
    return nc


_NC_CACHE = None


def kernel(L0_attn_x, L0_attn_QK, L0_attn_VO, L0_ffn_clip_x, L0_ffn_clip_down,
           L0_ffn_w_gate, L0_ffn_w_up, L0_ffn_w_down):
    global _NC_CACHE
    if _NC_CACHE is None:
        _NC_CACHE = build_nc()
    nc = _NC_CACHE
    x = np.asarray(L0_attn_x, dtype=np.float32)
    ins = {
        "wg": np.ascontiguousarray(np.asarray(L0_ffn_w_gate, dtype=np.float32)),
        "wu": np.ascontiguousarray(np.asarray(L0_ffn_w_up, dtype=np.float32)),
        "wd": np.ascontiguousarray(np.asarray(L0_ffn_w_down, dtype=np.float32)),
        "qkw": np.ascontiguousarray(np.asarray(L0_attn_QK, dtype=np.float32)),
        "vow": np.ascontiguousarray(np.asarray(L0_attn_VO, dtype=np.float32)),
        "clx": np.asarray(L0_ffn_clip_x, dtype=np.float32).reshape(1, 1),
        "cld": np.asarray(L0_ffn_clip_down, dtype=np.float32).reshape(1, 1),
    }
    in_maps = [dict(ins, xin=np.ascontiguousarray(x[b].T)) for b in range(NCORE)]
    res = run_bass_kernel_spmd(nc, in_maps, core_ids=list(range(NCORE)))
    out = np.empty((NCORE, T, D), dtype=np.float32)
    for b in range(NCORE):
        out[b] = res.results[b]["y"].T
    return out


# version 34
# speedup vs baseline: 1.0145x; 1.0134x over previous
import contextlib
import numpy as np
import concourse.bass as bass
import concourse.mybir as mybir
from concourse.alu_op_type import AluOpType
from concourse.bass_utils import run_bass_kernel_spmd

F32 = mybir.dt.float32
BF16 = mybir.dt.bfloat16
I8 = mybir.dt.int8
AF = mybir.ActivationFunctionType
AX = mybir.AxisListType.X
Mul = AluOpType.mult
Add = AluOpType.add
Sub = AluOpType.subtract
MaxO = AluOpType.max

NCORE = 8
D = 1024
T = 2048
IFF = 2816
DC = D // 128
NJ = DC
NI = IFF // 128
NTT = T // 128
NT4 = T // 512

ENGINES = ("sync", "gpsimd", "tensor", "vector", "scalar")
OWNED = {"sP": "tensor", "sV": "vector", "sA": "scalar"}


class Sched:
    def __init__(self):
        self.ops = {e: [] for e in ENGINES}
        self.cnt = {s: 0 for s in ("sL", "sS", "sG", "sP", "sV", "sA", "sC", "sW0", "sW1", "sW2", "sW3")}
        self.psum_read = {b: None for b in range(8)}

    def op(self, eng, fn, waits=(), inc=None, amt=1, drain=False, dma=False):
        waits = tuple(w for w in waits if w is not None)
        tok = None
        if inc is not None:
            self.cnt[inc] += amt
            tok = (inc, self.cnt[inc])
            if eng == "scalar" and not dma:
                drain = True
        self.ops[eng].append((fn, waits, inc, amt, drain))
        return tok

    def tok(self, sem):
        return (sem, self.cnt[sem])

    def pe(self, fn, waits=(), banks=(), drain=True):
        w = list(waits) + [self.psum_read[b] for b in banks]
        return self.op("tensor", fn, waits=w, inc="sP", drain=drain)

    def rd(self, eng, fn, waits=(), banks=(), inc=None, drain=False):
        if inc is None:
            inc = {"vector": "sV", "scalar": "sA"}[eng]
        t = self.op(eng, fn, waits=waits, inc=inc, drain=drain)
        for b in banks:
            self.psum_read[b] = t
        return t

    def vtiny(self, fn, waits=(), inc="sV"):
        return self.op("vector", fn, waits=waits, inc=inc, drain=True)

    def emit(self, block, sems):
        def section(eng_name):
            def run(eng):
                waited = {}
                for fn, waits, inc, amt, drain in self.ops[eng_name]:
                    for (s, v) in waits:
                        if OWNED.get(s) == eng_name:
                            continue
                        if waited.get(s, -1) >= v:
                            continue
                        eng.wait_ge(sems[s], v)
                        waited[s] = v
                    ins = fn(eng)
                    if drain:
                        eng.drain()
                    if inc is not None:
                        if ins is not None and not drain:
                            ins.then_inc(sems[inc], amt)
                        else:
                            eng.nop().then_inc(sems[inc], amt)
            return run

        for name in ENGINES:
            if self.ops[name]:
                getattr(block, name)(section(name))


def build_nc(debug=False):
    nc = bass.Bass("TRN2", target_bir_lowering=False)

    xin = nc.dram_tensor("xin", [D, T], F32, kind="ExternalInput")
    wg = nc.dram_tensor("wg", [IFF, D], F32, kind="ExternalInput")
    wu = nc.dram_tensor("wu", [IFF, D], F32, kind="ExternalInput")
    wd = nc.dram_tensor("wd", [D, IFF], F32, kind="ExternalInput")
    qkw = nc.dram_tensor("qkw", [D, D], F32, kind="ExternalInput")
    vow = nc.dram_tensor("vow", [D, D], F32, kind="ExternalInput")
    clx = nc.dram_tensor("clx", [1, 1], F32, kind="ExternalInput")
    cld = nc.dram_tensor("cld", [1, 1], F32, kind="ExternalInput")
    y = nc.dram_tensor("y", [D, T], F32, kind="ExternalOutput")
    wgq2 = nc.dram_tensor("wgq2", [NI, 128, D], BF16)
    wuq2 = nc.dram_tensor("wuq2", [NI, 128, D], BF16)
    wdq2 = nc.dram_tensor("wdq2", [NJ, 128, IFF], BF16)
    cin = nc.dram_tensor("cin", [128, 1], F32)
    cout = nc.dram_tensor("cout", [128, 1], F32)
    gsc = nc.dram_tensor("gsc", [1, 1], F32)
    if debug:
        dY1 = nc.dram_tensor("dY1", [D, T], F32, kind="ExternalOutput")
        dY2 = nc.dram_tensor("dY2", [D, T], F32, kind="ExternalOutput")
        dSC = nc.dram_tensor("dSC", [128, 8], F32, kind="ExternalOutput")
        dRM = nc.dram_tensor("dRM", [128, 80], F32, kind="ExternalOutput")
        dSG = nc.dram_tensor("dSG", [128, 66], F32, kind="ExternalOutput")

    S = Sched()

    with contextlib.ExitStack() as ctx:
        ec = ctx.enter_context
        xT = ec(nc.sbuf_tensor("xT", [128, DC, T], F32))
        lnb = ec(nc.sbuf_tensor("lnb", [128, DC, T], BF16))
        slab = ec(nc.sbuf_tensor("slab", [128, 52736], BF16))
        i8b = ec(nc.sbuf_tensor("i8b", [128, 2, IFF], I8))
        sgw = ec(nc.sbuf_tensor("sgw", [128, 3, 22], F32))
        amaxc = ec(nc.sbuf_tensor("amaxc", [128, 32], F32))
        vecs = ec(nc.sbuf_tensor("vecs", [128, 16], F32))
        ones128 = ec(nc.sbuf_tensor("ones128", [128, 128], BF16))
        ident = ec(nc.sbuf_tensor("ident", [128, 128], BF16))
        crow = ec(nc.sbuf_tensor("crow", [128, 128], F32))
        sml = ec(nc.sbuf_tensor("sml", [128, 16], F32))
        rmw = ec(nc.sbuf_tensor("rmw", [128, 80], F32))
        invw = ec(nc.sbuf_tensor("invw", [128, 64], F32))
        dbgt = ec(nc.sbuf_tensor("dbgt", [128, 8], F32)) if debug else None
        PS = ec(nc.psum_tensor("PS", [128, 8, 512], F32))
        semL = ec(nc.semaphore("sL"))
        semG = ec(nc.semaphore("sG"))
        semP = ec(nc.semaphore("sP"))
        semV = ec(nc.semaphore("sV"))
        semA = ec(nc.semaphore("sA"))
        semC = ec(nc.semaphore("sC"))
        semS = ec(nc.semaphore("sS"))
        semW = [ec(nc.semaphore("sW%d" % i)) for i in range(4)]
        block = ec(nc.Block())
        sems = {"sL": semL, "sG": semG, "sP": semP, "sV": semV, "sA": semA, "sC": semC,
                "sS": semS,
                "sW0": semW[0], "sW1": semW[1], "sW2": semW[2], "sW3": semW[3]}

        def sl(off, n):
            return slab[:, off:off + n]

        hv = [sl(k * T, T) for k in range(NI)]
        ln_tok = sl(0, 16384).rearrange("p (u d) -> p u d", d=1024)
        qc = sl(16384, 4096).rearrange("p (e m) -> p e m", m=512)
        Praw = sl(20480, 2048)
        Pn = sl(22528, 2048)
        PT = sl(24576, 8192).rearrange("p (u m) -> p u m", m=512)
        avbuf = sl(32768, 4096).rearrange("p (d m) -> p d m", m=512)
        xbf = sl(36864, 4096).rearrange("p (s m) -> p s m", s=4)
        x2bf = sl(40960, 4096).rearrange("p (s m) -> p s m", s=4)
        wwin_gu = sl(45056, 4096).rearrange("p (b w m) -> p b w m", b=2, w=2)
        wwin_d = sl(45056, 5632).rearrange("p (b m) -> p b m", b=2)
        wwin_qk = sl(45056, 4096).rearrange("p (b m) -> p b m", b=4)
        MB = sl(45056, 2048).bitcast(F32)
        RB = sl(47104, 2048).bitcast(F32)
        t32 = sl(49152, 2048).bitcast(F32)
        sgf2 = sl(50688, 2048).bitcast(F32)
        sgf = [sgf2[:, 0:512], sgf2[:, 512:1024]]

        dbg = sml
        c1024 = sml[:, 1:2]
        epsc = sml[:, 2:3]
        sigx = sml[:, 3:4]
        sigd = sml[:, 4:5]
        invq = sml[:, 5:6]
        sclq = sml[:, 6:7]
        c32 = sml[:, 7:8]
        hscl = sml[:, 8:9]
        tmp1 = sml[:, 9:10]
        tmp2 = sml[:, 10:11]

        S.op("vector", lambda v: v.memset(sml[:, :], 0.0))
        S.op("vector", lambda v: v.memset(c1024, 1.0 / 1024.0))
        S.op("vector", lambda v: v.memset(epsc, 1e-5))
        S.op("vector", lambda v: v.memset(c32, 1.0 / 32.0))
        S.op("vector", lambda v: v.memset(ones128[:, :], 1.0))
        S.op("gpsimd", lambda g: g.memset(ident[:, :], 0.0))
        t_id = S.op("gpsimd", lambda g: g.affine_select(
            out=ident[:, :], in_=ident[:, :], compare_op=AluOpType.not_equal,
            fill=1.0, base=0, pattern=[[-1, 128]], channel_multiplier=1), inc="sC")

        t_cl = S.op("gpsimd", lambda g: g.dma_start(
            out=sml[:, 11:12], in_=bass.AP(clx, 0, [[0, 128], [1, 1]])), inc="sG", amt=16)
        t_cl2 = S.op("gpsimd", lambda g: g.dma_start(
            out=sml[:, 12:13], in_=bass.AP(cld, 0, [[0, 128], [1, 1]])), inc="sG", amt=16)
        sig_tok_x = S.op("scalar", lambda s: s.activation(sigx, sml[:, 11:12], AF.Sigmoid),
                         waits=[t_cl], inc="sA", drain=True)
        sig_tok_d = S.op("scalar", lambda s: s.activation(sigd, sml[:, 12:13], AF.Sigmoid),
                         waits=[t_cl2], inc="sA", drain=True)

        t_x = [S.op("sync", lambda s, c=c: s.dma_start(
            out=xT[:, c, :], in_=xin[128 * c:128 * (c + 1), :]), inc="sL", amt=16)
            for c in range(DC)]

        state = {"tmp_guard": None, "coll_guard": None, "ffn_act_guard": None}
        prep_ctr = [0]
        prep_slot_guard = {}
        sgwf = sgw.ap().rearrange("p a b -> p (a b)")

        def emit_prep():
            pk = rmw
            pk2 = invw
            prep_bank = [0]
            slot_guard = {}

            def quant_tail(n, ld, cols, mult_ap, dst_ap, sc_tok):
                slot = n % 8
                base = 4096 * slot
                qb = sl(base + 2048, 1024)
                tb = sl(base + 3072, 1024)
                g = slot_guard.get(slot, (None, None, None))
                i8s = n % 2
                S.op("scalar", lambda s, ld=ld, cols=cols, i8s=i8s, mult_ap=mult_ap: s.activation(
                    i8b[:, i8s, 0:cols], ld[:, 0:cols], AF.Identity,
                    bias=0.0, scale=mult_ap),
                    waits=[sc_tok, state.get(("i8_guard", i8s))], drain=True)
                t_b = S.op("scalar", lambda s, qb=qb, cols=cols, i8s=i8s: s.activation(
                    qb[:, 0:cols], i8b[:, i8s, 0:cols], AF.Identity),
                    waits=[g[1]], inc="sA", drain=True)
                state[("i8_guard", i8s)] = t_b
                bank = 4 + (prep_bank[0] % 4)
                prep_bank[0] += 1
                pb = PS[:, bank, :].bitcast(BF16)
                t_tr = S.pe(lambda t, qb=qb, pb=pb, nblk=cols // 128: [
                    t.transpose(pb[:, 128 * m:128 * (m + 1)],
                                qb[:, 128 * m:128 * (m + 1)], ident[:, :])
                    for m in range(cols // 128)][-1],
                    waits=[t_b, t_id], banks=(bank,))
                t_ev = S.rd("vector", lambda v, tb=tb, cols=cols, pb=pb: v.tensor_copy(
                    tb[:, 0:cols], pb[:, 0:cols]),
                    waits=[t_tr, g[2]], banks=(bank,), drain=True)
                t_s = S.op("sync", lambda sy, dst_ap=dst_ap, tb=tb, cols=cols: sy.dma_start(
                    out=dst_ap, in_=tb[:, 0:cols]), waits=[t_ev], inc="sL", amt=16)
                slot_guard[slot] = (t_s, t_tr, t_s)
                return t_s

            def load(n, src_ap, cols):
                slot = n % 8
                ld = sl(4096 * slot, 2048).bitcast(F32)
                g = slot_guard.get(slot, (None, None, None))
                t_l = S.op("gpsimd", lambda gp, src_ap=src_ap, ld=ld, cols=cols: gp.dma_start(
                    out=ld[:, 0:cols], in_=src_ap), waits=[g[0]],
                    inc="sW%d" % (n % 4), amt=16, dma=True)
                return ld, t_l

            gate_spill = [None] * NI
            up_spill = [None] * NI
            n = 0
            for kp in range(11):
                items = []
                for kk in range(2):
                    k = 2 * kp + kk
                    for widx, wsrc in ((0, wg), (1, wu)):
                        col = 2 * k + widx
                        ld, t_l = load(n + len(items), wsrc[128 * k:128 * (k + 1), :], 1024)
                        t_r = S.op("vector", lambda v, ld=ld, col=col: v.reduce_max(
                            pk[:, col:col + 1], ld[:, 0:1024], axis=AX,
                            apply_absolute_value=True),
                            waits=[t_l], inc="sV", drain=(len(items) == 3))
                        items.append((ld, col, t_r))
                c0 = 4 * kp
                S.op("vector", lambda v, c0=c0: v.reciprocal(
                    pk2[:, c0:c0 + 4], pk[:, c0:c0 + 4]),
                    waits=[items[-1][2]], inc="sV", drain=True)
                sc_tok = S.op("vector", lambda v, c0=c0: v.tensor_scalar(
                    pk2[:, c0:c0 + 4], pk2[:, c0:c0 + 4], 127.0, 0.0, Mul, Add),
                    inc="sV", drain=True)
                for idx, (ld, col, _) in enumerate(items):
                    k = 2 * kp + idx // 2
                    widx = idx % 2
                    dst = (wgq2 if widx == 0 else wuq2).ap()[k, :, :]
                    t_s = quant_tail(n + idx, ld, 1024, pk2[:, col:col + 1], dst, sc_tok)
                    if widx == 0:
                        gate_spill[k] = t_s
                    else:
                        up_spill[k] = t_s
                n += 4

            down_spill = []
            for j in range(NJ):
                items = []
                for cc in range(3):
                    w = min(1024, IFF - 1024 * cc)
                    col = 44 + 3 * j + cc
                    ld, t_l = load(n + cc, wd[128 * j:128 * (j + 1), 1024 * cc:1024 * cc + w], w)
                    t_r = S.op("vector", lambda v, ld=ld, col=col, w=w: v.reduce_max(
                        pk[:, col:col + 1], ld[:, 0:w], axis=AX,
                        apply_absolute_value=True),
                        waits=[t_l], inc="sV", drain=(cc == 2))
                    items.append((ld, w, t_r))
                S.vtiny(lambda v, j=j: v.reduce_max(
                    pk[:, 68 + j:69 + j], pk[:, 44 + 3 * j:47 + 3 * j], axis=AX),
                    waits=[items[-1][2]])
                S.op("vector", lambda v, j=j: v.reciprocal(
                    pk2[:, 44 + j:45 + j], pk[:, 68 + j:69 + j]), inc="sV", drain=True)
                sc_tok = S.op("vector", lambda v, j=j: v.tensor_scalar(
                    pk2[:, 44 + j:45 + j], pk2[:, 44 + j:45 + j], 127.0, 0.0, Mul, Add),
                    inc="sV", drain=True)
                last = None
                for cc, (ld, w, _) in enumerate(items):
                    last = quant_tail(n + cc, ld, w, pk2[:, 44 + j:45 + j],
                                      wdq2.ap()[j, :, 1024 * cc:1024 * cc + w], sc_tok)
                n += 3
                down_spill.append(last)
            gca = bass.AP(pk[:, :].tensor, 0, [[80, 128], [2, 22]])
            uca = bass.AP(pk[:, :].tensor, 1, [[80, 128], [2, 22]])
            S.op("vector", lambda v, gca=gca: v.tensor_scalar(
                sgwf[:, 0:22], gca, 1.0 / 127.0, 0.0, Mul, Add), inc="sV", drain=True)
            S.op("vector", lambda v, uca=uca: v.tensor_scalar(
                sgwf[:, 22:44], uca, 1.0 / 127.0, 0.0, Mul, Add), inc="sV", drain=True)
            S.op("vector", lambda v: v.tensor_scalar(
                sgwf[:, 44:52], pk[:, 68:76], 1.0 / 127.0, 0.0, Mul, Add),
                inc="sV", drain=True)
            return gate_spill, up_spill, down_spill


        def bounce_allreduce(src_ap, waits):
            t1 = S.op("gpsimd", lambda g: g.dma_start(out=cin[:, :], in_=src_ap),
                      waits=list(waits) + [state["coll_guard"]], inc="sG", amt=16)
            t2 = S.op("gpsimd", lambda g: g.collective_compute(
                "AllReduce", MaxO, replica_groups=[list(range(NCORE))],
                ins=[cin.ap()], outs=[cout.ap()]), waits=[t1], inc="sC")
            state["coll_guard"] = t2
            t3 = S.op("gpsimd", lambda g: g.dma_start(
                out=crow[:, :], in_=bass.AP(cout, 0, [[0, 128], [1, 128]])),
                waits=[t2, state["tmp_guard"]], inc="sG", amt=16)
            t4 = S.vtiny(lambda v: v.reduce_max(tmp1, crow[:, :], axis=AX),
                         waits=[t3])
            return t4

        def quant_scales(sig_ap, gmax_tok, sig_tok, dbg_col=None):
            t1 = S.vtiny(lambda v, sig_ap=sig_ap: v.tensor_tensor(
                tmp2, tmp1, sig_ap, Mul), waits=[gmax_tok, sig_tok])
            state["tmp_guard"] = t1
            S.vtiny(lambda v: v.tensor_scalar(sclq, tmp2, 1.0 / 127.0, 0.0, Mul, Add))
            t3 = S.vtiny(lambda v: v.reciprocal(invq, sclq))
            if debug and dbg_col is not None:
                S.op("vector", lambda v, c=dbg_col: v.tensor_copy(
                    dbgt[:, c:c + 1], sclq), inc="sV")
                S.op("vector", lambda v, c=dbg_col: v.tensor_copy(
                    dbgt[:, c + 1:c + 2], tmp1), inc="sV")
            return t3

        def layer_norm(xtoks, with_amax):
            apply_toks = [None] * DC
            xslot_guard = {}
            x2slot_guard = {}
            for half in range(2):
                h0 = 1024 * half
                mm_x_last = mm_x2_last = None
                for c in range(DC):
                    slot = c % 4
                    t_cast = S.op("vector", lambda v, c=c, slot=slot, h0=h0: v.tensor_copy(
                        xbf[:, slot, :], xT[:, c, h0:h0 + 1024]),
                        waits=list(xtoks) + [xslot_guard.get(slot)], inc="sV", drain=True)
                    t_sq = S.op("scalar", lambda s, c=c, slot=slot, h0=h0: s.activation(
                        x2bf[:, slot, :], xT[:, c, h0:h0 + 1024], AF.Square),
                        waits=list(xtoks) + [x2slot_guard.get(slot)], inc="sA")
                    mm_x_last = S.pe(lambda t, c=c, slot=slot: [
                        t.matmul(PS[:, q, :], ones128[:, :], xbf[:, slot, 512 * q:512 * (q + 1)],
                                 start=(c == 0), stop=(c == DC - 1)) for q in range(2)][-1],
                        waits=[t_cast], banks=(0, 1) if c == 0 else ())
                    xslot_guard[slot] = mm_x_last
                    mm_x2_last = S.pe(lambda t, c=c, slot=slot: [
                        t.matmul(PS[:, 2 + q, :], ones128[:, :], x2bf[:, slot, 512 * q:512 * (q + 1)],
                                 start=(c == 0), stop=(c == DC - 1)) for q in range(2)][-1],
                        waits=[t_sq], banks=(2, 3) if c == 0 else ())
                    x2slot_guard[slot] = mm_x2_last
                ps_x = PS[:, 0:2, :].rearrange("p a b -> p (a b)")
                ps_x2 = PS[:, 2:4, :].rearrange("p a b -> p (a b)")
                t_mb = S.rd("vector", lambda v, ps_x=ps_x: v.tensor_scalar(
                    MB[:, :], ps_x, c1024, 0.0, Mul, Add),
                    waits=[mm_x_last], banks=(0, 1))
                S.op("vector", lambda v: v.tensor_tensor(t32[:, :], MB[:, :], MB[:, :], Mul),
                     inc="sV")
                t_var = S.rd("vector", lambda v, ps_x2=ps_x2: v.scalar_tensor_tensor(
                    RB[:, :], ps_x2, c1024, t32[:, :], Mul, Sub),
                    waits=[mm_x2_last], banks=(2, 3))
                t_sd = S.op("scalar", lambda s: s.activation(
                    t32[:, :], RB[:, :], AF.Sqrt, bias=epsc, scale=1.0),
                    waits=[t_var], inc="sA")
                t_rb = S.op("vector", lambda v: v.reciprocal(RB[:, :], t32[:, :]),
                            waits=[t_sd], inc="sV")
                for c in range(DC):
                    S.op("vector", lambda v, c=c, h0=h0: v.tensor_tensor(
                        t32[:, :], xT[:, c, h0:h0 + 1024], MB[:, :], Sub),
                        waits=[t_rb], inc="sV")
                    tb = S.op("vector", lambda v, c=c, h0=h0: v.tensor_tensor(
                        lnb[:, c, h0:h0 + 1024], t32[:, :], RB[:, :], Mul), inc="sV",
                        drain=(c == DC - 1) or half == 1)
                    if half == 1:
                        apply_toks[c] = tb
            if not with_amax:
                return apply_toks, None
            am = []
            for c in range(DC):
                am.append(S.op("vector", lambda v, c=c: v.reduce_max(
                    amaxc[:, c:c + 1], lnb[:, c, :], axis=AX, apply_absolute_value=True),
                    waits=[apply_toks[c]], inc="sV", drain=True))
            S.vtiny(lambda v: v.reduce_max(tmp2, amaxc[:, 0:DC], axis=AX), waits=am)
            t_am = S.vtiny(lambda v: v.tensor_scalar(
                vecs[:, 15:16], tmp2, 1.0, 0.0, Mul, Add))
            return apply_toks, t_am

        def ffn_ln(xtoks):
            return layer_norm(xtoks, with_amax=True)

        def ffn_front(xtoks, dbg_base=None, ln_pre=None):
            if ln_pre is None:
                apply_toks, t_am = layer_norm(xtoks, with_amax=True)
            else:
                apply_toks, t_am = ln_pre
            gmax = bounce_allreduce(vecs[:, 15:16], [t_am])
            t_scale = quant_scales(sigx, gmax, sig_tok_x,
                                   dbg_col=dbg_base)
            xq_toks = []
            for c in range(DC):
                i8s = c % 2
                S.op("scalar", lambda s, c=c, i8s=i8s: s.activation(
                    i8b[:, i8s, 0:T], lnb[:, c, :], AF.Identity,
                    bias=0.0, scale=invq),
                    waits=[t_scale, state.get(("i8_guard", i8s))], drain=True)
                ta = S.op("scalar", lambda s, c=c, i8s=i8s: s.activation(
                    lnb[:, c, :], i8b[:, i8s, 0:T], AF.Identity),
                    inc="sA", drain=True)
                state[("i8_guard", i8s)] = ta
                xq_toks.append(ta)
            return apply_toks, t_scale, xq_toks

        def ffn(xtoks, gate_sp, up_sp, down_sp, dbg_base=None, pre=None):
            if pre is None:
                apply_toks, t_scale, xq_toks = ffn_front(xtoks, dbg_base)
            else:
                apply_toks, t_scale, xq_toks = pre
            t_sx = S.vtiny(lambda v: v.tensor_scalar(
                vecs[:, 14:15], sclq, 1.0, 0.0, Mul, Add), waits=[t_scale])
            t_guv = S.op("vector", lambda v: v.tensor_scalar(
                invw[:, 0:44], sgwf[:, 0:44], vecs[:, 14:15], 0.0, Mul, Add),
                waits=[t_sx, state.get("ffn_act_guard")], inc="sV", drain=True)

            hv_toks = [None] * NI
            init_guard = [state.get("wwin_guard"), apply_toks[-1], t_scale]
            win_guard = {0: list(init_guard), 1: list(init_guard)}
            am_toks = []
            for k in range(NI):
                buf = k % 2
                wtok = []
                for w, (wsrc, sp) in enumerate(((wgq2, gate_sp), (wuq2, up_sp))):
                    wtok.append(S.op("sync", lambda s, k=k, w=w, buf=buf, wsrc=wsrc: s.dma_start(
                        out=wwin_gu[:, buf, w, :],
                        in_=wsrc.ap()[k, :, :]),
                        waits=[sp[k]] + win_guard[buf], inc="sL", amt=16))

                last_mm = None
                for half in range(2):
                    b0 = 4 * ((2 * k + half) % 2)
                    banks = (b0, b0 + 1, b0 + 2, b0 + 3)
                    t_mm = S.pe(lambda t, k=k, half=half, b0=b0, buf=buf: [
                        t.matmul(PS[:, b0 + 2 * w + tc, :],
                                 wwin_gu[:, buf, w, 128 * h:128 * (h + 1)],
                                 lnb[:, h, 1024 * half + 512 * tc:1024 * half + 512 * (tc + 1)],
                                 start=(h == 0), stop=(h == DC - 1))
                        for w in range(2) for h in range(DC) for tc in range(2)][-1],
                        waits=list(xq_toks) + wtok, banks=banks)
                    last_mm = t_mm
                    t_sis = []
                    for tc in range(2):
                        t_sis.append(S.rd("scalar", lambda s, k=k, b0=b0, tc=tc: s.activation(
                            sgf[tc][:, :], PS[:, b0 + tc, :], AF.Silu,
                            bias=0.0, scale=invw[:, k:k + 1]),
                            waits=[t_mm, t_guv, state.get(("sg_guard", tc))],
                            banks=(b0 + tc,), drain=(tc == 1)))
                    for tc in range(2):
                        t0 = 1024 * half + 512 * tc
                        t_hv = S.rd("vector", lambda v, k=k, b0=b0, tc=tc, t0=t0: v.scalar_tensor_tensor(
                            hv[k][:, t0:t0 + 512], PS[:, b0 + 2 + tc, :],
                            invw[:, 22 + k:23 + k], sgf[tc][:, :], Mul, Mul),
                            waits=[t_sis[tc], t_guv], banks=(b0 + 2 + tc,))
                        state[("sg_guard", tc)] = t_hv
                        hv_toks[k] = t_hv
                win_guard[buf] = [last_mm]
                am_toks.append(S.op("vector", lambda v, k=k: v.reduce_max(
                    amaxc[:, 8 + k:9 + k], hv[k][:, :], axis=AX, apply_absolute_value=True),
                    waits=[hv_toks[k]], inc="sV", drain=True))

            am = am_toks
            S.vtiny(lambda v: v.reduce_max(tmp2, amaxc[:, 8:8 + NI], axis=AX), waits=am)
            t_am2 = S.vtiny(lambda v: v.tensor_scalar(
                vecs[:, 15:16], tmp2, 1.0, 0.0, Mul, Add))
            gmax2 = bounce_allreduce(vecs[:, 15:16], [t_am2])
            t_scale2 = quant_scales(sigd, gmax2, sig_tok_d,
                                    dbg_col=None if dbg_base is None else dbg_base + 2)
            t_hs = S.vtiny(lambda v: v.tensor_scalar(
                hscl, sclq, 0.5, 0.0, Mul, Add), waits=[t_scale2])
            t_dv = S.op("vector", lambda v: v.tensor_scalar(
                invw[:, 44:52], sgwf[:, 44:52], hscl, 0.0, Mul, Add),
                waits=[t_hs], inc="sV", drain=True)

            hq_toks = []
            for k in range(NI):
                i8s = k % 2
                tq = S.op("vector", lambda v, k=k, i8s=i8s: v.tensor_scalar(
                    i8b[:, i8s, 0:T], hv[k][:, :], invq, 0.0, Mul, Add),
                    waits=[t_scale2, state.get(("i8_guard", i8s))], inc="sV")
                ta = S.op("scalar", lambda s, k=k, i8s=i8s: s.activation(
                    hv[k][:, :], i8b[:, i8s, 0:T], AF.Identity), waits=[tq],
                    inc="sA", drain=True)
                state[("i8_guard", i8s)] = ta
                hq_toks.append(ta)

            xt_done = [None] * DC
            dg0 = win_guard[0] + win_guard[1]
            dwin_guard = {0: dg0, 1: list(dg0)}
            last_mm = None
            for j in range(NJ):
                buf = j % 2
                wt = [S.op("sync", lambda s, j=j, buf=buf: s.dma_start(
                    out=wwin_d[:, buf, :], in_=wdq2.ap()[j, :, :]),
                    waits=[down_sp[j]] + dwin_guard[buf],
                    inc="sL", amt=16)]
                b0 = 4 * (j % 2)
                S.pe(lambda t, j=j, buf=buf, b0=b0: [
                    t.matmul(PS[:, b0 + tc, :],
                             wwin_d[:, buf, 128 * i:128 * (i + 1)],
                             hv[i][:, 512 * tc:512 * (tc + 1)],
                             start=(i == 0), stop=False)
                    for tc in range(NT4) for i in range(6)][-1],
                    waits=hq_toks[0:6] + wt,
                    banks=(b0, b0 + 1, b0 + 2, b0 + 3), drain=False)
                S.pe(lambda t, j=j, buf=buf, b0=b0: [
                    t.matmul(PS[:, b0 + tc, :],
                             wwin_d[:, buf, 128 * i:128 * (i + 1)],
                             hv[i][:, 512 * tc:512 * (tc + 1)],
                             start=False, stop=False)
                    for tc in range(NT4) for i in range(6, 14)][-1],
                    waits=hq_toks[0:14], banks=(), drain=False)
                t_mm = S.pe(lambda t, j=j, buf=buf, b0=b0: [
                    t.matmul(PS[:, b0 + tc, :],
                             wwin_d[:, buf, 128 * i:128 * (i + 1)],
                             hv[i][:, 512 * tc:512 * (tc + 1)],
                             start=False, stop=(i == NI - 1))
                    for tc in range(NT4) for i in range(14, NI)][-1],
                    waits=hq_toks, banks=(), drain=True)
                last_mm = t_mm
                for tc in range(NT4):
                    bank = b0 + tc
                    sg = tc % 2
                    t_st = S.rd("vector", lambda v, j=j, tc=tc, bank=bank, sg=sg: v.scalar_tensor_tensor(
                        sgf[sg][:, :], PS[:, bank, :], invw[:, 44 + j:45 + j],
                        xT[:, j, 512 * tc:512 * (tc + 1)], Mul, Add),
                        waits=[t_mm, t_dv, state.get(("sg_guard", sg))], banks=(bank,))
                    t_cp = S.op("vector", lambda v, j=j, tc=tc, sg=sg: v.tensor_copy(
                        xT[:, j, 512 * tc:512 * (tc + 1)], sgf[sg][:, :]), inc="sV")
                    state[("sg_guard", sg)] = t_cp
                    xt_done[j] = t_cp
                dwin_guard[buf] = [last_mm]
            state["wwin_guard"] = last_mm
            state["ffn_act_guard"] = hq_toks[-1]
            return xt_done

        def attention(xtoks):
            apply_toks, _ = layer_norm(xtoks, with_amax=False)
            lt_toks = []
            for up in range(NTT // 2):
                t_tr = S.pe(lambda t, up=up: [
                    t.transpose(PS[:, 4 + uu, :].bitcast(BF16)[:, 128 * c:128 * (c + 1)],
                                lnb[:, c, 128 * (2 * up + uu):128 * (2 * up + uu) + 128],
                                ident[:, :])
                    for uu in range(2) for c in range(DC)][-1],
                    waits=list(apply_toks) + [t_id], banks=(4, 5))
                for uu in range(2):
                    pb = PS[:, 4 + uu, :].bitcast(BF16)
                    lt_toks.append(S.rd("vector", lambda v, up=up, uu=uu, pb=pb: v.tensor_copy(
                        ln_tok[:, 2 * up + uu, :], pb[:, :]), waits=[t_tr], banks=(4 + uu,),
                        drain=True))

            xt_done = [None] * DC
            ig = [state.get("wwin_guard"), apply_toks[-1]]
            qk_guard = {"init": list(ig)}
            vo_guard = {"init": list(ig)}
            for ch in range(NT4):
                t0 = 512 * ch
                q_toks = []
                mid_tok = None
                for d in range(DC):
                    wslot = d % 4
                    gw = qk_guard["init"] if d < 4 else [mid_tok]
                    t_w = S.op("gpsimd", lambda g, d=d, wslot=wslot: g.dma_start(
                        out=wwin_qk[:, wslot, :], in_=qkw[128 * d:128 * (d + 1), :]),
                        waits=gw, inc="sG", amt=16)
                    t_mm = S.pe(lambda t, d=d, t0=t0, wslot=wslot: [
                        t.matmul(PS[:, e, :], wwin_qk[:, wslot, 128 * e:128 * (e + 1)],
                                 lnb[:, d, t0:t0 + 512],
                                 start=(d == 0), stop=(d == DC - 1))
                        for e in range(DC)][-1],
                        waits=[t_w] + list(apply_toks),
                        banks=tuple(range(8)) if d == 0 else (),
                        drain=(d == 3 or d == DC - 1))
                    if d == 3:
                        mid_tok = t_mm
                    mmq_toks = t_mm
                for e in range(DC):
                    q_toks.append(S.rd("vector", lambda v, e=e: v.tensor_scalar(
                        qc[:, e, :], PS[:, e, :], c32, 0.0, Mul, Add),
                        waits=[mmq_toks], banks=(e,), drain=(e == DC - 1)))
                qk_guard["init"] = [mmq_toks]

                pt_toks = []
                s_toks = {}

                def emit_S(lt):
                    b0 = 4 * (lt % 2)
                    s_toks[lt] = S.pe(lambda t, lt=lt, b0=b0: [
                        t.matmul(PS[:, b0 + uc, :], qc[:, e, 128 * lt:128 * (lt + 1)],
                                 lnb[:, e, 512 * uc:512 * (uc + 1)],
                                 start=(e == 0), stop=(e == DC - 1))
                        for e in range(DC) for uc in range(4)][-1],
                        waits=q_toks, banks=(b0, b0 + 1, b0 + 2, b0 + 3))

                emit_S(0)
                emit_S(1)
                for lt in range(4):
                    b0 = 4 * (lt % 2)
                    banks4 = (b0, b0 + 1, b0 + 2, b0 + 3)
                    ps_s = PS[:, b0:b0 + 4, :].rearrange("p a b -> p (a b)")
                    S.rd("vector", lambda v, ps_s=ps_s: v.reduce_max(tmp1, ps_s, axis=AX),
                         waits=[s_toks[lt], state["tmp_guard"]], banks=banks4, drain=True)
                    t_ng = S.vtiny(lambda v: v.tensor_scalar(
                        tmp2, tmp1, -1.0, 0.0, Mul, Add))
                    state["tmp_guard"] = t_ng
                    t_exp = None
                    for uc in range(4):
                        t_exp = S.rd("scalar", lambda s, uc=uc, b0=b0: s.activation(
                            Praw[:, 512 * uc:512 * (uc + 1)], PS[:, b0 + uc, :], AF.Exp,
                            bias=tmp2, scale=1.0),
                            waits=[t_ng, state.get("praw_guard")], banks=(b0 + uc,))
                    t_sum = S.vtiny(lambda v: v.reduce_sum(
                        tmp1, Praw[:, :], axis=AX), waits=[t_exp])
                    state["tmp_guard"] = t_sum
                    S.vtiny(lambda v: v.reciprocal(tmp2, tmp1))
                    t_pn = S.op("vector", lambda v: v.tensor_scalar(
                        Pn[:, :], Praw[:, :], tmp2, 0.0, Mul, Add),
                        waits=[state.get("pn_guard")], inc="sV")
                    state["praw_guard"] = t_pn
                    tr_banks = (b0, b0 + 1)
                    t_tr = S.pe(lambda t, b0=b0: [
                        t.transpose(PS[:, b0 + (m // 8), :].bitcast(BF16)[:, 128 * (m % 8):128 * (m % 8) + 128],
                                    Pn[:, 128 * m:128 * m + 128], ident[:, :])
                        for m in range(16)][-1],
                        waits=[t_pn, t_id], banks=tr_banks)
                    evs = []
                    for half in range(2):
                        pb = PS[:, b0 + half, :].bitcast(BF16)
                        evs.append(S.rd("vector", lambda v, half=half, lt=lt, pb=pb: v.tensor_copy(
                            PT[:, 8 * half:8 * half + 8, 128 * lt:128 * lt + 128],
                            pb.rearrange("p (a b) -> p a b", a=8)),
                            waits=[t_tr, state.get("pt_guard")], banks=(b0 + half,),
                            drain=(half == 1)))
                    pt_toks += evs
                    state["pn_guard"] = evs[-1]
                    if lt + 2 < 4:
                        emit_S(lt + 2)

                t_mm_av = None
                for d in range(DC):
                    t_mm_av = S.pe(lambda t, d=d: [
                        t.matmul(PS[:, d, :], ln_tok[:, u, 128 * d:128 * (d + 1)],
                                 PT[:, u, :], start=(u == 0), stop=(u == NTT - 1))
                        for u in range(NTT)][-1],
                        waits=pt_toks + lt_toks if d == 0 else [],
                        banks=(d,), drain=(d == DC - 1))
                av_toks = []
                for d in range(DC):
                    av_toks.append(S.rd("vector", lambda v, d=d: v.tensor_copy(
                        avbuf[:, d, :], PS[:, d, :]), waits=[t_mm_av], banks=(d,),
                        drain=(d == DC - 1)))
                state["pt_guard"] = av_toks[-1]

                t_mm_o = None
                vmid = None
                for d in range(DC):
                    wslot = d % 4
                    gw = vo_guard["init"] if d < 4 else [vmid]
                    t_w = S.op("gpsimd", lambda g, d=d, wslot=wslot: g.dma_start(
                        out=wwin_qk[:, wslot, :], in_=vow[128 * d:128 * (d + 1), :]),
                        waits=gw + [av_toks[d]], inc="sG", amt=16)
                    t_mm_o = S.pe(lambda t, d=d, wslot=wslot: [
                        t.matmul(PS[:, e, :], wwin_qk[:, wslot, 128 * e:128 * (e + 1)],
                                 avbuf[:, d, :], start=(d == 0), stop=(d == DC - 1))
                        for e in range(DC)][-1],
                        waits=[t_w] + (av_toks if d == 0 else []),
                        banks=tuple(range(8)) if d == 0 else (),
                        drain=(d == 3 or d == DC - 1))
                    if d == 3:
                        vmid = t_mm_o
                vo_guard["init"] = [t_mm_o]
                qk_guard["init"] = [t_mm_o]
                state["wwin_guard"] = t_mm_o
                for e in range(DC):
                    sg = e % 2
                    t_st = S.rd("vector", lambda v, e=e, t0=t0, sg=sg: v.tensor_tensor(
                        sgf[sg][:, :], PS[:, e, :], xT[:, e, t0:t0 + 512], Add),
                        waits=[t_mm_o, state.get(("sg_guard", sg)), xtoks[e]], banks=(e,))
                    t_cp = S.op("vector", lambda v, e=e, t0=t0, sg=sg: v.tensor_copy(
                        xT[:, e, t0:t0 + 512], sgf[sg][:, :]), inc="sV")
                    state[("sg_guard", sg)] = t_cp
                    if ch == NT4 - 1:
                        xt_done[e] = t_cp
            return xt_done

        dbg_sink = []

        def dump(dst, src_tiles, waits):
            for c, (ap, tok) in enumerate(zip(src_tiles, waits)):
                dbg_sink.append(S.op("gpsimd", lambda g, c=c, ap=ap, dst=dst: g.dma_start(
                    out=dst[128 * c:128 * (c + 1), :], in_=ap), waits=[tok],
                    inc="sG", amt=16))

        ln1 = ffn_ln(t_x)
        gate_spill, up_spill, down_spill = emit_prep()
        pre1 = ffn_front(t_x, dbg_base=0 if debug else None, ln_pre=ln1)
        xt1 = ffn(t_x, gate_spill, up_spill, down_spill, pre=pre1)
        if debug:
            dump(dY1, [xT[:, c, :] for c in range(DC)], xt1)
        xt2 = attention(xt1)
        if debug:
            dump(dY2, [xT[:, c, :] for c in range(DC)], xt2)
        xt3 = ffn(xt2, gate_spill, up_spill, down_spill,
                  dbg_base=4 if debug else None)
        if debug:
            dbg_sink.append(S.op("gpsimd", lambda g: g.dma_start(
                out=dSC[:, :], in_=dbgt[:, :]), waits=[S.tok("sV")], inc="sG", amt=16))
            dbg_sink.append(S.op("gpsimd", lambda g: g.dma_start(
                out=dRM[:, :], in_=rmw[:, :]), waits=[t_inv], inc="sG", amt=16))
            dbg_sink.append(S.op("gpsimd", lambda g: g.dma_start(
                out=dSG[:, :], in_=sgwf[:, :]), waits=[t_inv], inc="sG", amt=16))

        st_toks = [S.op("sync", lambda s, c=c: s.dma_start(
            out=y[128 * c:128 * (c + 1), :], in_=xT[:, c, :]),
            waits=[xt3[c]], inc="sL", amt=16) for c in range(DC)]
        S.op("gpsimd", lambda g: g.nop(), waits=st_toks)

        S.emit(block, sems)
    return nc


_NC_CACHE = None


def kernel(L0_attn_x, L0_attn_QK, L0_attn_VO, L0_ffn_clip_x, L0_ffn_clip_down,
           L0_ffn_w_gate, L0_ffn_w_up, L0_ffn_w_down):
    global _NC_CACHE
    if _NC_CACHE is None:
        _NC_CACHE = build_nc()
    nc = _NC_CACHE
    x = np.asarray(L0_attn_x, dtype=np.float32)
    ins = {
        "wg": np.ascontiguousarray(np.asarray(L0_ffn_w_gate, dtype=np.float32)),
        "wu": np.ascontiguousarray(np.asarray(L0_ffn_w_up, dtype=np.float32)),
        "wd": np.ascontiguousarray(np.asarray(L0_ffn_w_down, dtype=np.float32)),
        "qkw": np.ascontiguousarray(np.asarray(L0_attn_QK, dtype=np.float32)),
        "vow": np.ascontiguousarray(np.asarray(L0_attn_VO, dtype=np.float32)),
        "clx": np.asarray(L0_ffn_clip_x, dtype=np.float32).reshape(1, 1),
        "cld": np.asarray(L0_ffn_clip_down, dtype=np.float32).reshape(1, 1),
    }
    in_maps = [dict(ins, xin=np.ascontiguousarray(x[b].T)) for b in range(NCORE)]
    res = run_bass_kernel_spmd(nc, in_maps, core_ids=list(range(NCORE)))
    out = np.empty((NCORE, T, D), dtype=np.float32)
    for b in range(NCORE):
        out[b] = res.results[b]["y"].T
    return out
